# version 16
# baseline (speedup 1.0000x reference)
import numpy as np
from contextlib import ExitStack

import concourse.bass as bass
import concourse.mybir as mybir
from concourse.bass_utils import run_bass_kernel_spmd
from concourse.alu_op_type import AluOpType as OP

F32 = mybir.dt.float32
BF16 = mybir.dt.bfloat16
AF = mybir.ActivationFunctionType
AX = mybir.AxisListType.X

ENGINES = ("q", "t", "v", "s", "g")
SCALE32 = float(np.float32(1.0) / np.sqrt(np.float32(128.0)))


class Rec:

    def __init__(self):
        self.steps = []
        from collections import defaultdict
        self.cnt = defaultdict(int)
        self.floor = {e: [] for e in ENGINES}

    def op(self, eng, fn, waits=(), inc=None, inc_n=1):
        w = list(self.floor[eng]) + [x for x in waits if x is not None and x[1] is not None]
        self.floor[eng] = []
        self.steps.append((eng, fn, w, inc, inc_n))
        if inc is not None:
            self.cnt[inc] += inc_n
            return self.cnt[inc]
        return None

    def load(self, out, in_, waits=(), ch="0"):
        sn = "ld_" + ch
        return (sn, self.op("q", lambda e: e.dma_start(out=out, in_=in_), waits, sn, 16))

    def store(self, out, in_, waits=(), ch="0"):
        sn = "st_" + ch
        return (sn, self.op("s", lambda e: e.dma_start(out=out, in_=in_), waits, sn, 16))

    def dma_sems(self, pfx):
        return [n for n in self.cnt if n.startswith(pfx)]

    def pe(self, fn, waits=()):
        return self.op("t", fn, waits, "pe", 1)

    def dve(self, fn, waits=()):
        w = list(waits)
        if self.cnt["dve"] > 0:
            w.append(("dve", self.cnt["dve"]))
        return self.op("v", fn, w, "dve", 1)

    def act(self, fn, waits=()):
        w = list(waits)
        if self.cnt["act"] > 0:
            w.append(("act", self.cnt["act"]))
        return self.op("s", fn, w, "act", 1)

    def pool(self, fn, waits=()):
        w = list(waits)
        if self.cnt["pool"] > 0:
            w.append(("pool", self.cnt["pool"]))
        return self.op("g", fn, w, "pool", 1)

    def barrier(self):
        lds = [(n, self.cnt[n]) for n in list(self.cnt) if n.startswith("ld_")]
        sts = [(n, self.cnt[n]) for n in list(self.cnt) if n.startswith("st_")]
        self.op("q", lambda e: e.nop(), lds + sts, "ph", 1)
        self.op("t", lambda e: e.drain(), [], "ph", 1)
        self.op("v", lambda e: e.drain(), [], "ph", 1)
        self.op("s", lambda e: e.drain(), sts, "ph", 1)
        self.op("g", lambda e: e.drain(), [], "ph", 1)
        tgt = self.cnt["ph"]
        for e in ENGINES:
            self.floor[e] = [("ph", tgt)]


def build_nc(S=2048, D=2048, F=8192, debug_taps=()):
    nC = S // 128
    nD = D // 128
    nH = F // 128
    nTs = S // 512
    n_heads = nD
    DQ = 2 * D
    TOKQ = 512
    nTokq = S // TOKQ
    D2G = min(4, nD)
    nD2g = nD // D2G

    nc = bass.Bass("TRN2", target_bir_lowering=False)

    def dram(name, shape, kind="Internal", dt=F32):
        k = "ExternalOutput" if name in debug_taps else kind
        return nc.dram_tensor(name, shape, dt, kind=k).ap()

    x_in = dram("x_in", [S, D], "ExternalInput")
    w1 = dram("w1", [F, D], "ExternalInput")
    w2 = dram("w2", [D, F], "ExternalInput")
    wq = dram("wq", [D, DQ], "ExternalInput")
    wk = dram("wk", [D, DQ], "ExternalInput")
    wv = dram("wv", [D, D], "ExternalInput")
    wo = dram("wo", [D, D], "ExternalInput")
    ns_in = dram("ns", [D], "ExternalInput")
    lam_in = dram("lam", [n_heads], "ExternalInput")
    y_out = dram("y_out", [S, D], "ExternalOutput")

    t1t_d = dram("t1t_d", [nH, 128, D], dt=BF16)
    t2t_d = dram("t2t_d", [nD2g, 128, D2G * nH * 128], dt=BF16)
    g_d = dram("g_d", [nH, 128, S])
    m_d = dram("m_d", [nTokq, nH, 128, TOKQ], dt=BF16)
    x1_d = dram("x1_d", [S, D])
    x2_d = dram("x2_d", [S, D])
    qs_d = dram("qs_d", [DQ, S])
    ks_d = dram("ks_d", [DQ, S])
    vs_d = dram("vs_d", [D, S])
    oh_d = dram("oh_d", [D, S])
    ntdbg_d = dram("ntdbg_d", [128, nD * S], dt=BF16)

    R = Rec()
    ctx = ExitStack()
    sem = {}

    sb = lambda n, sh, dt=F32: ctx.enter_context(nc.sbuf_tensor(n, sh, dt))
    idb = sb("idb", [128, 128], BF16)
    idf = sb("idf", [128, 128], F32)
    ones1 = sb("ones1", [1, 128], F32)
    ns_cols = sb("ns_cols", [128, nD], F32)
    nsf_cols = sb("nsf_cols", [128, nD], F32)
    lam_row = sb("lam_row", [1, n_heads], F32)
    lam_cols = sb("lam_cols", [128, n_heads], F32)
    facb = sb("facb", [128, 1], F32)
    s1_all = sb("s1_all", [128, nH], F32)
    s1g_all = sb("s1g_all", [128, nH], F32)
    s2_all = sb("s2_all", [128, nD], F32)
    isx_cols = sb("isx_cols", [128, nC], F32)
    ish_cols = sb("ish_cols", [128, nC], F32)
    rstd_cols = sb("rstd_cols", [128, nC], F32)
    tiny = sb("tiny", [128, 16], F32)
    tiny2 = sb("tiny2", [128, 16], F32)
    rowbuf = sb("rowbuf", [1, S], F32)
    cpool = sb("cpool", [128, 4], F32)
    trrow = sb("trrow", [128, 128], F32)
    _CONSTS = (1e-5, 1e-6, -128.0)
    for _i, _v in enumerate(_CONSTS):
        nc.const_aps.register(F32, _v, cpool[:, _i:_i + 1])

    for _i, _v in enumerate(_CONSTS):
        R.pool(lambda e, _i=_i, _v=_v: e.memset(cpool[:, _i:_i + 1], _v))
    c_ns = ("ld_cns", R.op("q", lambda e: e.dma_start(out=ns_cols[:, :],
                                           in_=ns_in.rearrange("(a b) -> b a", b=128),
                                           allow_slow_non_contiguous=True), [], "ld_cns", 16))
    c_lam = R.load(lam_row[0:1, :], lam_in[:], ch="clam")
    R.pool(lambda e: e.memset(ones1[:, :], 1.0))
    R.pool(lambda e: e.memset(idb[:, :], 0.0))
    R.pool(lambda e: e.affine_select(out=idb[:, :], in_=idb[:, :], compare_op=OP.not_equal,
                                     fill=1.0, base=0, pattern=[[-1, 128]], channel_multiplier=1))
    R.pool(lambda e: e.memset(idf[:, :], 0.0))
    p_id = R.pool(lambda e: e.affine_select(out=idf[:, :], in_=idf[:, :], compare_op=OP.not_equal,
                                            fill=1.0, base=0, pattern=[[-1, 128]], channel_multiplier=1))
    R.dve(lambda e: e.memset(trrow[:, :], 0.0))
    d_ml = R.dve(lambda e: e.reduce_sum(tiny[0:1, 0:1], lam_row[0:1, :], axis=AX), [c_lam])
    d_ml = R.dve(lambda e: e.tensor_scalar(tiny[0:1, 0:1], tiny[0:1, 0:1], -1.0 / n_heads, 1.0,
                                           OP.mult, OP.add))
    with nc.psum_tensor("ps_init", [128, 16], F32) as psi:
        p_b = R.pe(lambda e: e.matmul(psi[:, 0:1], ones1[0:1, 0:128], tiny[0:1, 0:1],
                                      start=True, stop=True), [("pool", p_id), ("dve", d_ml)])
        d_f = R.dve(lambda e: e.tensor_copy(facb[:, :], psi[:, 0:1]), [("pe", p_b)])
        p_b2 = R.pe(lambda e: e.matmul(psi[:, 0:n_heads], ones1[0:1, 0:128], lam_row[0:1, :],
                                       start=True, stop=True), [("dve", d_f)])
        d_lc = R.dve(lambda e: e.tensor_copy(lam_cols[:, :], psi[:, 0:n_heads]), [("pe", p_b2)])
        R.dve(lambda e: e.tensor_scalar(nsf_cols[:, :], ns_cols[:, :], facb[:, 0:1], None, OP.mult),
              [c_ns])
    R.barrier()

    def ladder_pair(bufs):
        offs = 0.0
        last_act = [None] * len(bufs)
        last_dve = [b["z_ready"] for b in bufs]
        for k in (128.0, 64.0, 32.0, 16.0, 8.0, 4.0, 2.0, 1.0):
            bias = -(k + offs)
            for bi, b in enumerate(bufs):
                u, sg = b["u"], b["sgn"]
                last_act[bi] = R.act(
                    lambda e, u=u, sg=sg, bias=bias: e.activation(sg, u, AF.Sign, bias=bias, scale=1.0),
                    [("dve", last_dve[bi])])
            for bi, b in enumerate(bufs):
                u, sg = b["u"], b["sgn"]
                last_dve[bi] = R.dve(
                    lambda e, u=u, sg=sg, k=k: e.scalar_tensor_tensor(u, sg, -k / 2.0, u, OP.mult, OP.add),
                    [("act", last_act[bi])])
            offs += k / 2.0
        outs = []
        for bi, b in enumerate(bufs):
            u, zk, m = b["u"], b["zkeep"], b["out_m"]
            zoff = b.get("zoff", 0.0)
            R.dve(lambda e, u=u, zk=zk: e.scalar_tensor_tensor(u, u, -1.0, zk, OP.mult, OP.add))
            d = R.dve(lambda e, u=u, m=m, zoff=zoff: e.tensor_scalar(m, u, 1.0, zoff - 0.5,
                                                                     OP.mult, OP.add))
            outs.append(d)
        return outs

    def layernorm_chunk(xt, xn, sq, mean, var, rstd, x_ready):
        R.dve(lambda e: e.reduce_sum(mean, xt, axis=AX), [x_ready])
        d1 = R.dve(lambda e: e.tensor_scalar(mean, mean, 1.0 / D, None, OP.mult))
        a1 = R.act(lambda e: e.activation(sq, xt, AF.Square, accum_out=var), [x_ready])
        R.dve(lambda e: e.tensor_scalar(var, var, 1.0 / D, None, OP.mult), [("act", a1)])
        R.dve(lambda e: e.tensor_tensor(sq[:, 0:1], mean, mean, OP.mult))
        d2 = R.dve(lambda e: e.tensor_tensor(var, var, sq[:, 0:1], OP.subtract))
        a2 = R.act(lambda e: e.activation(rstd, var, AF.Sqrt, bias=1e-5, scale=1.0), [("dve", d2)])
        R.dve(lambda e: e.reciprocal(rstd, rstd), [("act", a2)])
        dn = R.dve(lambda e: e.tensor_scalar(xn, xt, mean, rstd, OP.subtract, OP.mult))
        return dn

    def ffn_phase(src, dst, first):
        if first:
            ternarize()
        with ExitStack() as cph:
            pfx = "A" if first else "B"
            bsh = cph.enter_context(nc.sbuf_tensor(pfx + "bsh", [128, S], F32))
            with ExitStack() as ca:
                nT = ca.enter_context(nc.sbuf_tensor(pfx + "nT", [128, nD * S], BF16))
                bisx = ca.enter_context(nc.sbuf_tensor(pfx + "bisx", [128, S], F32))
                ffn_ln_quant(src, nT, bisx, ca, pfx)
                if "ntdbg_d" in debug_taps and first:
                    R.store(ntdbg_d[:, :], nT[:, :], [("dve", R.cnt["dve"])], ch="dbg")
                R.barrier()
                ffn_mm1(nT, bisx, bsh, ca, pfx)
                R.barrier()
            ffn_mm2(src, dst, bsh, cph, pfx)
        R.barrier()

    def ternarize():
        with ExitStack() as c2:
            sb2 = lambda n, sh, dt=F32: c2.enter_context(nc.sbuf_tensor(n, sh, dt))
            wb = [sb2(f"p0w{i}", [128, D]) for i in range(2)]
            cb = sb2("p0c", [128, D], BF16)
            cb2 = sb2("p0c2", [128, D], BF16)
            tb = sb2("p0t", [128, D], BF16)
            tsp = [sb2(f"p0sp{i}", [128, D], BF16) for i in range(2)]
            shf = sb2("p0shf", [128, 2])
            with nc.psum_tensor("p0ps", [128, 4096], BF16) as pst:
                used = [None, None]
                spilled = [None, None]
                slot_ev = [None] * 4
                for r in range(nH):
                    i = r % 2
                    cw = R.load(wb[i][:, :], w1[r * 128:(r + 1) * 128, :],
                                [("dve", used[i])] if used[i] is not None else [], ch=f"p0w{i}")
                    R.dve(lambda e, i=i: e.reduce_sum(shf[:, 0:1], wb[i][:, :], axis=AX,
                                                      apply_absolute_value=True), [cw])
                    R.dve(lambda e, r=r: e.tensor_scalar(s1_all[:, r:r + 1], shf[:, 0:1], 1.0 / D, 1e-5,
                                                         OP.mult, OP.max))
                    d_s = R.dve(lambda e, r=r: e.tensor_scalar(shf[:, 0:1], s1_all[:, r:r + 1], 0.5, None, OP.mult))
                    d_s = R.dve(lambda e: e.tensor_scalar(shf[:, 1:2], shf[:, 0:1], -1.0, None, OP.mult))
                    aA = R.act(lambda e, i=i: e.activation(cb[:, :], wb[i][:, :], AF.Sign,
                                                           bias=shf[:, 1:2], scale=1.0), [cw, ("dve", d_s)])
                    aB = R.act(lambda e, i=i: e.activation(cb2[:, :], wb[i][:, :], AF.Sign,
                                                           bias=shf[:, 0:1], scale=1.0), [cw])
                    dt_ = R.dve(lambda e: e.tensor_tensor(tb[:, :], cb[:, :], cb2[:, :], OP.add),
                                [("act", aB)])
                    R.dve(lambda e, r=r: e.tensor_scalar(s1g_all[:, r:r + 1], s1_all[:, r:r + 1],
                                                         0.5, None, OP.mult))
                    used[i] = dt_
                    j = r % 2
                    last = None
                    pstv = pst.rearrange("p (c x) -> p c x", x=1024)
                    tspv = tsp[j].rearrange("p (c x) -> p c x", x=128)
                    for d4 in range(0, nD, 4):
                        ng4 = min(4, nD - d4)
                        for k in range(ng4):
                            dch = d4 + k
                            wt = [("dve", dt_)] if dch == 0 else []
                            if k == 0 and last is not None:
                                wt.append(("dve", last))
                            if dch == 0 and spilled[j] is not None:
                                wt.append(spilled[j])
                            pt = R.pe(lambda e, k=k, dch=dch: e.matmul(
                                pstv[:, k, 0:128],
                                tb[:, dch * 128:(dch + 1) * 128], idb[:, :],
                                is_transpose=True, start=True, stop=True), wt)
                        last = R.dve(lambda e, d4=d4, ng4=ng4, tspv=tspv: e.tensor_copy(
                            tspv[:, d4:d4 + ng4, 0:128], pstv[:, 0:ng4, 0:128]), [("pe", pt)])
                    spilled[j] = R.store(t1t_d[r, :, :], tsp[j][:, :], [("dve", last)], ch=f"p0sp{j}")
        R.barrier()
        with ExitStack() as c2:
            sb2 = lambda n, sh, dt=F32: c2.enter_context(nc.sbuf_tensor(n, sh, dt))
            wb_w2 = [sb2(f"p1w{i}", [128, F]) for i in range(2)]
            cb_w2 = sb2("p1c", [128, F], BF16)
            cb2_w2 = sb2("p1c2", [128, F], BF16)
            tb_w2 = sb2("p1t", [128, F], BF16)
            tsp_w2 = [sb2("p1sp0", [128, nH * 128], BF16)]
            shf_w2 = sb2("p1shf", [128, 2])
            with nc.psum_tensor("p1ps", [128, 4096], BF16) as pst_w2:
                used_w2 = [None, None]
                spilled_w2 = [None]
                slot_ev2 = [None] * 4
                for r in range(nD):
                    i = r % 2
                    cw_w2 = R.load(wb_w2[i][:, :], w2[r * 128:(r + 1) * 128, :],
                                [("dve", used_w2[i])] if used_w2[i] is not None else [], ch=f"p1w{i}")
                    R.dve(lambda e, i=i: e.reduce_sum(shf_w2[:, 0:1], wb_w2[i][:, :], axis=AX,
                                                      apply_absolute_value=True), [cw_w2])
                    R.dve(lambda e, r=r: e.tensor_scalar(s2_all[:, r:r + 1], shf_w2[:, 0:1], 1.0 / F, 1e-5,
                                                         OP.mult, OP.max))
                    d_s = R.dve(lambda e, r=r: e.tensor_scalar(shf_w2[:, 0:1], s2_all[:, r:r + 1], 0.5, None, OP.mult))
                    d_s = R.dve(lambda e: e.tensor_scalar(shf_w2[:, 1:2], shf_w2[:, 0:1], -1.0, None, OP.mult))
                    aA = R.act(lambda e, i=i: e.activation(cb_w2[:, :], wb_w2[i][:, :], AF.Sign,
                                                           bias=shf_w2[:, 1:2], scale=1.0), [cw_w2, ("dve", d_s)])
                    aB = R.act(lambda e, i=i: e.activation(cb2_w2[:, :], wb_w2[i][:, :], AF.Sign,
                                                           bias=shf_w2[:, 0:1], scale=1.0), [cw_w2])
                    dt__w2 = R.dve(lambda e: e.tensor_tensor(tb_w2[:, :], cb_w2[:, :], cb2_w2[:, :], OP.add),
                                   [("act", aB)])
                    used_w2[i] = dt__w2
                    j = 0
                    qtr, jj = r // D2G, r % D2G
                    last_w2 = None
                    pstv2 = pst_w2.rearrange("p (c x) -> p c x", x=1024)
                    tspv2 = tsp_w2[j].rearrange("p (c x) -> p c x", x=128)
                    for h4 in range(0, nH, 4):
                        ng4 = min(4, nH - h4)
                        for k in range(ng4):
                            hc = h4 + k
                            wt_w2 = [("dve", dt__w2)] if hc == 0 else []
                            if k == 0 and last_w2 is not None:
                                wt_w2.append(("dve", last_w2))
                            if hc == 0 and spilled_w2[j] is not None:
                                wt_w2.append(spilled_w2[j])
                            pt_w2 = R.pe(lambda e, k=k, hc=hc: e.matmul(
                                pstv2[:, k, 0:128],
                                tb_w2[:, hc * 128:(hc + 1) * 128], idb[:, :],
                                is_transpose=True, start=True, stop=True), wt_w2)
                        last_w2 = R.dve(lambda e, h4=h4, ng4=ng4, tspv2=tspv2: e.tensor_copy(
                            tspv2[:, h4:h4 + ng4, 0:128], pstv2[:, 0:ng4, 0:128]), [("pe", pt_w2)])
                    spilled_w2[j] = R.store(t2t_d[qtr, :, jj * nH * 128:(jj + 1) * nH * 128],
                                         tsp_w2[j][:, :], [("dve", last_w2)], ch=f"p1sp{j}")
        R.barrier()

    def ffn_ln_quant(src, nT, bisx, cph, pfx):
        with ExitStack() as c2:
            sb2 = lambda n, sh, dt=F32: c2.enter_context(nc.sbuf_tensor(pfx + n, sh, dt))
            xt = [sb2(f"lx{i}", [128, D]) for i in range(2)]
            xn = [sb2(f"ln{i}", [128, D]) for i in range(2)]
            sq = xn
            zk = [sb2(f"lz{i}", [128, D]) for i in range(2)]
            sg = [sb2(f"lg{i}", [128, D]) for i in range(2)]
            nb = [sb2(f"lb{i}", [128, D], BF16) for i in range(2)]
            mv = sb2("lmv", [128, 8])
            with nc.psum_tensor(pfx + "lps", [128, 4096], BF16) as pst:
                used = [None, None]
                slot_ev = [None] * 4
                for c0 in range(0, nC, 2):
                    pair = [c for c in (c0, c0 + 1) if c < nC]
                    bufs = []
                    for c in pair:
                        i = c % 2
                        cx = R.load(xt[i][:, :], src[c * 128:(c + 1) * 128, :],
                                    [("dve", used[i])] if used[i] is not None else [], ch=f"lnx{i}")
                        dn = layernorm_chunk(xt[i][:, :], xn[i][:, :], sq[i][:, :],
                                             mv[:, 4 * i:4 * i + 1], mv[:, 4 * i + 1:4 * i + 2],
                                             mv[:, 4 * i + 2:4 * i + 3], cx)
                        q = mv[:, 4 * i + 3:4 * i + 4]
                        R.dve(lambda e, i=i, q=q: e.reduce_max(q, xn[i][:, :], axis=AX,
                                                               apply_absolute_value=True))
                        R.dve(lambda e, q=q: e.tensor_scalar(q, q, 1e-5, None, OP.max))
                        R.dve(lambda e, q=q, c=c: e.tensor_scalar(isx_cols[:, c:c + 1], q,
                                                                  1.0 / 127.0, None, OP.mult))
                        R.dve(lambda e, q=q: e.reciprocal(q, q))
                        R.dve(lambda e, q=q: e.tensor_scalar(q, q, 127.0, None, OP.mult))
                        R.dve(lambda e, i=i, q=q: e.tensor_scalar(zk[i][:, :], xn[i][:, :], q, 128.5,
                                                                  OP.mult, OP.add))
                        dz = R.dve(lambda e, i=i: e.tensor_copy(xn[i][:, :], zk[i][:, :]))
                        bufs.append(dict(u=xn[i][:, :], sgn=sg[i][:, :], zkeep=zk[i][:, :],
                                         out_m=nb[i][:, :], z_ready=dz, c=c, i=i))
                    outs = ladder_pair(bufs)
                    pstv = pst.rearrange("p (a x) -> p a x", x=1024)
                    nTv = nT.rearrange("p (a x) -> p a x", x=S)
                    for b, o in zip(bufs, outs):
                        c, i = b["c"], b["i"]
                        last = None
                        for d4 in range(0, nD, 4):
                            ng4 = min(4, nD - d4)
                            for k in range(ng4):
                                dch = d4 + k
                                wt = [("dve", o)] if dch == 0 else []
                                if k == 0 and slot_ev[0] is not None:
                                    wt.append(("dve", slot_ev[0]))
                                pt = R.pe(lambda e, i=i, k=k, dch=dch: e.matmul(
                                    pstv[:, k, 0:128],
                                    nb[i][:, dch * 128:(dch + 1) * 128], idb[:, :],
                                    is_transpose=True, start=True, stop=True), wt)
                            last = R.dve(lambda e, d4=d4, ng4=ng4, c=c: e.tensor_copy(
                                nTv[:, d4:d4 + ng4, c * 128:(c + 1) * 128],
                                pstv[:, 0:ng4, 0:128]), [("pe", pt)])
                            slot_ev[0] = last
                        used[i] = last
            dv = R.cnt["dve"]
            with nc.psum_tensor(pfx + "lpb", [128, 512], F32) as pb:
                dcp = R.dve(lambda e: e.tensor_copy(trrow[:, 0:nC], isx_cols[:, 0:nC]), [("dve", dv)])
                ptr = R.pe(lambda e: e.transpose(pb[:, 0:128], trrow[:, 0:128], idf[:, :]),
                           [("dve", dcp)])
                dtr = R.dve(lambda e: e.tensor_copy(trrow[:, 0:128], pb[:, 0:128]), [("pe", ptr)])
                cr = R.load(rowbuf[0:1, 0:S], trrow[0:nC, 0:128], [("dve", dtr)], ch="row")
                dl = None
                for tsl in range(nTs):
                    pbm = R.pe(lambda e, tsl=tsl: e.matmul(pb[:, 0:512],
                                                           ones1[0:1, 0:128],
                                                           rowbuf[0:1, tsl * 512:(tsl + 1) * 512],
                                                           start=True, stop=True),
                               [cr, ("dve", dtr)] + ([("dve", dl)] if dl is not None else []))
                    dl = R.dve(lambda e, tsl=tsl: e.tensor_copy(bisx[:, tsl * 512:(tsl + 1) * 512],
                                                                pb[:, 0:512]), [("pe", pbm)])

    def ffn_mm1(nT, bisx, bsh, cph, pfx):
        with ExitStack() as c2:
            sb2 = lambda n, sh, dt=F32: c2.enter_context(nc.sbuf_tensor(pfx + n, sh, dt))
            t1s = [sb2(f"mw{i}", [128, nD * 128], BF16) for i in range(2)]
            y2b = [sb2(f"my{i}", [128, S]) for i in range(2)]
            gb = [sb2(f"mg{i}", [128, S]) for i in range(2)]
            amx = sb2("mam", [128, S])
            amn = sb2("mam2", [128, S])
            psy = [c2.enter_context(nc.psum_tensor(pfx + f"mp{i}", [128, 2048], F32)) for i in range(2)]
            R.dve(lambda e: e.memset(amx[:, :], 0.0))
            R.dve(lambda e: e.memset(amn[:, :], 0.0))
            usedw = [None, None]
            evacd = [None, None]
            gf = [None, None]
            gst = [None, None]
            for hc in range(nH):
                i = hc % 2
                cw = R.load(t1s[i][:, :], t1t_d[hc, :, :],
                            [("pe", usedw[i])] if usedw[i] is not None else [], ch=f"m1w{i}")
                lastmm = None
                dlast = None
                for tsl in range(nTs):
                    for dch in range(nD):
                        wt = []
                        if dch == 0 and tsl == 0:
                            wt = [cw]
                            if evacd[i] is not None:
                                wt.append(("dve", evacd[i]))
                        lastmm = R.pe(lambda e, i=i, dch=dch, tsl=tsl: e.matmul(
                            psy[i][:, tsl * 512:(tsl + 1) * 512],
                            t1s[i][:, dch * 128:(dch + 1) * 128],
                            nT[:, dch * S + tsl * 512: dch * S + (tsl + 1) * 512],
                            start=(dch == 0), stop=(dch == nD - 1)), wt)
                    wt = [("pe", lastmm)]
                    if tsl == 0 and gf[i] is not None:
                        wt.append(("act", gf[i]))
                    dlast = R.dve(lambda e, i=i, tsl=tsl: e.tensor_tensor(
                        y2b[i][:, tsl * 512:(tsl + 1) * 512], psy[i][:, tsl * 512:(tsl + 1) * 512],
                        bisx[:, tsl * 512:(tsl + 1) * 512], OP.mult), wt)
                usedw[i] = lastmm
                evacd[i] = dlast
                wg = [("dve", dlast)]
                if gst[i] is not None:
                    wg.append(gst[i])
                ag = R.act(lambda e, i=i, hc=hc: e.activation(gb[i][:, :], y2b[i][:, :],
                                                              AF.Gelu_apprx_tanh,
                                                              scale=s1g_all[:, hc:hc + 1]), wg)
                gf[i] = ag
                R.dve(lambda e, i=i: e.tensor_tensor(amx[:, :], gb[i][:, :], amx[:, :], OP.max),
                      [("act", ag)])
                R.dve(lambda e, i=i: e.tensor_tensor(amn[:, :], gb[i][:, :], amn[:, :], OP.min))
                gst[i] = R.store(g_d[hc, :, :], gb[i][:, :], [("act", ag)], ch=f"m1g{i}")
            pt0 = psy[0]
            pt1 = psy[1]
            R.dve(lambda e: e.tensor_scalar(amn[:, :], amn[:, :], -1.0, None, OP.mult))
            R.dve(lambda e: e.tensor_tensor(amx[:, :], amx[:, :], amn[:, :], OP.max))
            dve_amax = R.cnt["dve"]
            dl = None
            for c in range(nC):
                wt = [("dve", dve_amax)] if c == 0 else []
                if dl is not None and c >= 4:
                    wt.append(("dve", dl))
                pm = R.pe(lambda e, c=c: e.transpose(pt0[:, (c % 4) * 512:(c % 4) * 512 + 128],
                                                     amx[:, c * 128:(c + 1) * 128], idf[:, :]), wt)
                R.dve(lambda e, c=c: e.reduce_max(tiny[:, 0:1],
                                                  pt0[:, (c % 4) * 512:(c % 4) * 512 + 128],
                                                  axis=AX), [("pe", pm)])
                R.dve(lambda e: e.tensor_scalar(tiny[:, 0:1], tiny[:, 0:1], 1e-5, None, OP.max))
                R.dve(lambda e, c=c: e.tensor_scalar(ish_cols[:, c:c + 1], tiny[:, 0:1],
                                                     1.0 / 127.0, None, OP.mult))
                R.dve(lambda e: e.reciprocal(tiny2[:, 0:1], tiny[:, 0:1]))
                dl = R.dve(lambda e, c=c: e.tensor_scalar(rstd_cols[:, c:c + 1], tiny2[:, 0:1],
                                                          127.0, None, OP.mult))
            dcp = R.dve(lambda e: e.tensor_copy(trrow[:, 0:nC], rstd_cols[:, 0:nC]), [("dve", dl)])
            ptr = R.pe(lambda e: e.transpose(pt1[:, 0:128], trrow[:, 0:128], idf[:, :]),
                       [("dve", dcp)])
            dtr = R.dve(lambda e: e.tensor_copy(trrow[:, 0:128], pt1[:, 0:128]), [("pe", ptr)])
            cr = R.load(rowbuf[0:1, 0:S], trrow[0:nC, 0:128], [("dve", dtr)], ch="row")
            dl = None
            for tsl in range(nTs):
                pbm = R.pe(lambda e, tsl=tsl: e.matmul(pt1[:, 0:512], ones1[0:1, 0:128],
                                                       rowbuf[0:1, tsl * 512:(tsl + 1) * 512],
                                                       start=True, stop=True),
                           [cr, ("dve", dtr)] + ([("dve", dl)] if dl is not None else []))
                dl = R.dve(lambda e, tsl=tsl: e.tensor_copy(bsh[:, tsl * 512:(tsl + 1) * 512],
                                                            pt1[:, 0:512]), [("pe", pbm)])

    def ffn_mm2(src, dst, bsh, cph, pfx):
        NG = 4
        ngrp = nH // NG
        with ExitStack() as c2:
            sb2 = lambda n, sh, dt=F32: c2.enter_context(nc.sbuf_tensor(pfx + n, sh, dt))
            t2s = c2.enter_context(nc.sbuf_tensor(pfx + "2w", [128, D2G * nH * 128], BF16))
            mbuf = c2.enter_context(nc.sbuf_tensor(pfx + "2m", [128, nH * TOKQ], BF16))
            gld = [sb2(f"2g{i}", [128, NG * TOKQ]) for i in range(2)]
            zkb = [sb2(f"2z{i}", [128, NG * TOKQ]) for i in range(2)]
            sgb = [sb2(f"2s{i}", [128, NG * TOKQ]) for i in range(2)]
            o2f = [sb2(f"2o{i}", [128, 512]) for i in range(2)]
            xtt = [sb2(f"2x{i}", [128, 512]) for i in range(2)]
            ott = [sb2(f"2t{i}", [128, 512]) for i in range(2)]
            psm = c2.enter_context(nc.psum_tensor(pfx + "2p", [128, D2G * 512], F32))
            pstr = c2.enter_context(nc.psum_tensor(pfx + "2pt", [128, 1024], F32))
            mg_ready = {}
            pe_at = {}
            st_grp = {}
            ott_st = [None] * 4
            xtt_use = [None] * 4
            o2f_use = [None, None]
            tr_ev = [None, None]
            ptr_use = [None, None]
            t2_used = None
            for qtr in range(nD2g):
                cw = R.load(t2s[:, :], t2t_d[qtr, :, :],
                            [("pe", t2_used)] if t2_used is not None else [], ch="m2w")
                for tq in range(nTokq):
                    if qtr == 0:
                        lastg = [None, None]
                        for g0 in range(0, ngrp, 2):
                            prs = []
                            for grp in (g0, g0 + 1):
                                if grp >= ngrp:
                                    continue
                                i = grp % 2
                                wt0 = [("dve", lastg[i])] if lastg[i] is not None else []
                                if tq > 0:
                                    wt0.append(("pe", pe_at[(tq - 1, grp)]))
                                    if (tq - 1, grp) in st_grp:
                                        wt0.append(st_grp[(tq - 1, grp)])
                                cg = None
                                for k in range(NG):
                                    hc = grp * NG + k
                                    cg = R.load(gld[i][:, k * TOKQ:(k + 1) * TOKQ],
                                                g_d[hc, :, tq * TOKQ:(tq + 1) * TOKQ],
                                                wt0 if k == 0 else [], ch=f"m2g{i}")
                                for k in range(NG):
                                    R.dve(lambda e, i=i, k=k, tq=tq: e.tensor_tensor(
                                        zkb[i][:, k * TOKQ:(k + 1) * TOKQ],
                                        gld[i][:, k * TOKQ:(k + 1) * TOKQ],
                                        bsh[:, tq * TOKQ:(tq + 1) * TOKQ], OP.mult),
                                        [cg] if k == 0 else [])
                                dz = R.dve(lambda e, i=i: e.tensor_scalar(gld[i][:, :], zkb[i][:, :],
                                                                          1.0, 128.5, OP.mult, OP.add))
                                prs.append(dict(u=gld[i][:, :], sgn=sgb[i][:, :], zkeep=zkb[i][:, :],
                                                out_m=mbuf[:, grp * NG * TOKQ:(grp + 1) * NG * TOKQ],
                                                z_ready=dz, grp=grp, i=i, zoff=128.5))
                            outs = ladder_pair(prs)
                            for pr, o in zip(prs, outs):
                                lastg[pr["i"]] = o
                                grp = pr["grp"]
                                mg_ready[(tq, grp)] = ("dve", o)
                                st_grp[(tq, grp)] = R.store(
                                    m_d[tq, grp * NG:(grp + 1) * NG, :, :],
                                    mbuf[:, grp * NG * TOKQ:(grp + 1) * NG * TOKQ],
                                    [("dve", o)], ch=f"m2m{grp}")
                    else:
                        for grp in range(ngrp):
                            prev_tq = tq - 1 if tq > 0 else nTokq - 1
                            wtl = [("pe", pe_at[(prev_tq, grp)])]
                            if qtr == 1:
                                wtl.append(st_grp[(nTokq - 1, grp)])
                            cm = R.load(mbuf[:, grp * NG * TOKQ:(grp + 1) * NG * TOKQ],
                                        m_d[tq, grp * NG:(grp + 1) * NG, :, :],
                                        wtl, ch=f"m2mr{grp}")
                            mg_ready[(tq, grp)] = cm
                    lastmm = None
                    for hc in range(nH):
                        for j in range(D2G):
                            wt = []
                            if hc % NG == 0 and j == 0:
                                wt = [mg_ready[(tq, hc // NG)]]
                                if hc == 0:
                                    wt.append(cw)
                                    if o2f_use[0] is not None:
                                        wt.append(("act", o2f_use[0]))
                                    if o2f_use[1] is not None:
                                        wt.append(("act", o2f_use[1]))
                            lastmm = R.pe(lambda e, hc=hc, j=j: e.matmul(
                                psm[:, j * 512:j * 512 + 512],
                                t2s[:, (j * nH + hc) * 128:(j * nH + hc) * 128 + 128],
                                mbuf[:, hc * TOKQ: hc * TOKQ + 512],
                                start=(hc == 0), stop=(hc == nH - 1)), wt)
                            if hc % NG == NG - 1 and j == D2G - 1:
                                pe_at[(tq, hc // NG)] = lastmm
                    t2_used = lastmm
                    for j in range(D2G):
                        d2c = qtr * D2G + j
                        ib = j % 2
                        tok0 = tq * TOKQ
                        wt = [("pe", lastmm)]
                        if tr_ev[ib] is not None:
                            wt.append(("dve", tr_ev[ib]))
                        a1 = R.act(lambda e, j=j, d2c=d2c, ib=ib: e.activation(
                            o2f[ib][:, :], psm[:, j * 512:j * 512 + 512], AF.Copy,
                            scale=s2_all[:, d2c:d2c + 1]), wt)
                        o2f_use[ib] = a1
                        cx = R.load(xtt[ib][:, :],
                                    src[tok0:tok0 + 512, d2c * 128:(d2c + 1) * 128]
                                    .rearrange("(a b) c -> b a c", b=128),
                                    [("dve", xtt_use[ib])] if xtt_use[ib] is not None else [],
                                    ch=f"m2x{ib}")
                        dub = None
                        for t4 in range(4):
                            c_tok = (tok0 + t4 * 128) // 128
                            sl = t4 % 2
                            pm = R.pe(lambda e, t4=t4, ib=ib, sl=sl: e.transpose(
                                pstr[:, sl * 512:sl * 512 + 128],
                                o2f[ib][:, t4 * 128:(t4 + 1) * 128], idf[:, :]),
                                [("act", a1)] + ([("dve", ptr_use[sl])] if ptr_use[sl] is not None else []))
                            wt2 = [("pe", pm)]
                            if t4 == 0:
                                wt2.append(cx)
                                if ott_st[ib] is not None:
                                    wt2.append(ott_st[ib])
                            d1 = R.dve(lambda e, t4=t4, ib=ib, c_tok=c_tok, sl=sl: e.tensor_scalar(
                                ott[ib][:, t4 * 128:(t4 + 1) * 128], pstr[:, sl * 512:sl * 512 + 128],
                                ish_cols[:, c_tok:c_tok + 1], 0.25, OP.mult, OP.mult), wt2)
                            ptr_use[sl] = d1
                            dub = R.dve(lambda e, t4=t4, ib=ib: e.tensor_tensor(
                                ott[ib][:, t4 * 128:(t4 + 1) * 128],
                                ott[ib][:, t4 * 128:(t4 + 1) * 128],
                                xtt[ib][:, t4 * 128:(t4 + 1) * 128], OP.add))
                        xtt_use[ib] = dub
                        tr_ev[ib] = dub
                        ott_st[ib] = R.store(dst[tok0:tok0 + 512, d2c * 128:(d2c + 1) * 128]
                                             .rearrange("(a b) c -> b a c", b=128),
                                             ott[ib][:, :], [("dve", dub)], ch=f"m2t{ib}")

    def attn_phase(src, dst):
        with ExitStack() as cph:
          with ExitStack() as cxn:
            xnTh = cxn.enter_context(nc.sbuf_tensor("at_xnTh", [128, nD * S], BF16))
            xnTl = cxn.enter_context(nc.sbuf_tensor("at_xnTl", [128, nD * S], BF16))
            with ExitStack() as c2:
                sb2 = lambda n, sh, dt=F32: c2.enter_context(nc.sbuf_tensor(n, sh, dt))
                xt = [sb2(f"a2x{i}", [128, D]) for i in range(2)]
                xn = [sb2(f"a2n{i}", [128, D]) for i in range(2)]
                sq = xn
                mv = sb2("a2mv", [128, 8])
                with nc.psum_tensor("a2ps", [128, 2048], F32) as pst:
                    used = [None, None]
                    slot_ev = [None] * 4
                    for c in range(nC):
                        i = c % 2
                        cx = R.load(xt[i][:, :], src[c * 128:(c + 1) * 128, :],
                                    [("dve", used[i])] if used[i] is not None else [], ch=f"a2x{i}")
                        dn = layernorm_chunk(xt[i][:, :], xn[i][:, :], sq[i][:, :],
                                             mv[:, 4 * i:4 * i + 1], mv[:, 4 * i + 1:4 * i + 2],
                                             mv[:, 4 * i + 2:4 * i + 3], cx)
                        dl = None
                        for dch in range(nD):
                            sl = dch % 4
                            wt = [("dve", dn)] if dch == 0 else []
                            if slot_ev[sl] is not None:
                                wt.append(("dve", slot_ev[sl]))
                            pm = R.pe(lambda e, i=i, sl=sl, dch=dch: e.transpose(
                                pst[:, sl * 512:sl * 512 + 128],
                                xn[i][:, dch * 128:(dch + 1) * 128], idf[:, :]), wt)
                            R.dve(lambda e, sl=sl, dch=dch, c=c: e.tensor_copy(
                                xnTh[:, dch * S + c * 128: dch * S + c * 128 + 128],
                                pst[:, sl * 512:sl * 512 + 128]), [("pe", pm)])
                            dl = R.dve(lambda e, sl=sl, dch=dch, c=c: e.scalar_tensor_tensor(
                                xnTl[:, dch * S + c * 128: dch * S + c * 128 + 128],
                                xnTh[:, dch * S + c * 128: dch * S + c * 128 + 128], -1.0,
                                pst[:, sl * 512:sl * 512 + 128], OP.mult, OP.add))
                            slot_ev[sl] = dl
                        used[i] = dl
            R.barrier()
            with ExitStack() as c2:
                sb2 = lambda n, sh, dt=F32: c2.enter_context(nc.sbuf_tensor(n, sh, dt))
                wc = [sb2(f"prw{i}", [128, nD * 128]) for i in range(2)]
                wch = [sb2(f"prh{i}", [128, nD * 128], BF16) for i in range(2)]
                wcl = [sb2(f"prl{i}", [128, nD * 128], BF16) for i in range(2)]
                ev = [sb2(f"prev{i}", [128, 512]) for i in range(2)]
                with nc.psum_tensor("prps", [128, 1024], F32) as psp:
                    evst = [None, None]
                    wc_used = [None, None]
                    wsplit = [None, None]
                    for W, out_d, width in ((wq, qs_d, DQ), (wk, ks_d, DQ), (wv, vs_d, D)):
                        for oc in range(width // 128):
                            iw = oc % 2
                            lw = None
                            for dc in range(nD):
                                lw = R.load(wc[iw][:, dc * 128:(dc + 1) * 128],
                                            W[dc * 128:(dc + 1) * 128, oc * 128:(oc + 1) * 128],
                                            [("dve", wsplit[iw])] if dc == 0 and wsplit[iw] is not None else [],
                                            ch=f"prw{iw}")
                            wt = [lw]
                            if wc_used[iw] is not None:
                                wt.append(("pe", wc_used[iw]))
                            R.dve(lambda e, iw=iw: e.tensor_copy(wch[iw][:, :], wc[iw][:, :]), wt)
                            wsplit[iw] = R.dve(lambda e, iw=iw: e.scalar_tensor_tensor(
                                wcl[iw][:, :], wch[iw][:, :], -1.0, wc[iw][:, :], OP.mult, OP.add))
                            for tsl in range(nTs):
                                b = tsl % 2
                                lastmm = None
                                for wop, xop in ((wch, xnTh), (wch, xnTl), (wcl, xnTh)):
                                    for dc in range(nD):
                                        wt = []
                                        if lastmm is None:
                                            wt = [("dve", wsplit[iw])]
                                            if evst[b] is not None:
                                                wt.append(("dve", evst[b][0]))
                                        first = lastmm is None
                                        lastmm = R.pe(lambda e, iw=iw, dc=dc, tsl=tsl, b=b,
                                                      wop=wop, xop=xop, first=first: e.matmul(
                                            psp[:, b * 512:b * 512 + 512],
                                            wop[iw][:, dc * 128:(dc + 1) * 128],
                                            xop[:, dc * S + tsl * 512: dc * S + (tsl + 1) * 512],
                                            start=first,
                                            stop=(wop is wcl and dc == nD - 1)), wt)
                                if tsl == nTs - 1:
                                    wc_used[iw] = lastmm
                                wt = [("pe", lastmm)]
                                if evst[b] is not None:
                                    wt.append(evst[b][1])
                                dl = R.dve(lambda e, b=b: e.tensor_copy(ev[b][:, :],
                                                                        psp[:, b * 512:b * 512 + 512]), wt)
                                st = R.store(out_d[oc * 128:(oc + 1) * 128, tsl * 512:(tsl + 1) * 512],
                                             ev[b][:, :], [("dve", dl)], ch=f"prev{b}")
                                evst[b] = (dl, st)
            R.barrier()
          if True:
            acc = cph.enter_context(nc.sbuf_tensor("at_acc", [128, S], F32))
            with ExitStack() as c2:
                sb2 = lambda n, sh, dt=F32: c2.enter_context(nc.sbuf_tensor(n, sh, dt))
                q1 = sb2("hq1", [128, S]); q2 = sb2("hq2", [128, S])
                k1 = sb2("hk1", [128, S]); k2 = sb2("hk2", [128, S])
                vv = sb2("hv", [128, S]); vT = sb2("hvT", [128, S])
                e1 = sb2("he1", [128, S]); e2 = sb2("he2", [128, S])
                wbp = [sb2(f"hw{i}", [128, S]) for i in range(2)]
                wT = sb2("hwT", [128, nC * 512])
                sqb = sb2("hsq", [128, 512])
                ob = [sb2(f"hob{i}", [128, 512]) for i in range(2)]
                obn = [sb2(f"hon{i}", [128, 512]) for i in range(2)]
                rr = sb2("hrr", [128, 8])
                PA = c2.enter_context(nc.psum_tensor("hpa", [128, 2048], F32))
                PB = c2.enter_context(nc.psum_tensor("hpb", [128, 2048], F32))
                PAv = PA.rearrange("p (c x) -> p c x", x=512)
                PBv = PB.rearrange("p (c x) -> p c x", x=512)
                wTv = wT.rearrange("p (c x) -> p c x", x=512)
                R.dve(lambda e: e.memset(acc[:, :], 0.0))
                obn_st = [None, None]
                ob_use = [None, None]
                pend = None
                a_e1_prev = None
                a_e2_prev = None
                o_ev = None
                wtr_ev = None
                wtr_pe = None
                omm_pe = {}
                head_gate = None

                def emit_wtr_and_omm():
                    nonlocal pend, wtr_ev, wtr_pe, o_ev
                    if pend is None:
                        return
                    h, qt, qg, dwc, wbuf = (pend[k] for k in ("h", "qt", "qg", "dw", "wbuf"))
                    for c4 in range(0, nC, 4):
                        for k in range(4):
                            c = c4 + k
                            wt = []
                            if c == 0:
                                wt = [("dve", dwc), ("act", a_e2_prev)]
                                gk = (h, qt - qg - 4) if qg == 0 else None
                                prevg = ((qt // 4) - 1)
                                if qg == 0 and (h, prevg) in omm_pe:
                                    pass
                            if k == 0 and wtr_ev is not None:
                                wt.append(wtr_ev)
                            pm = R.pe(lambda e, c=c, wbuf=wbuf: e.transpose(
                                PBv[:, c % 4, 0:128], wbuf[:, c * 128:(c + 1) * 128], idf[:, :]), wt)
                        wtr_pe = pm
                        wtr_ev = ("dve", R.dve(lambda e, c4=c4, qg=qg: e.tensor_copy(
                            wTv[:, c4:c4 + 4, qg * 128:qg * 128 + 128],
                            PBv[:, 0:4, 0:128]), [("pe", pm)]))
                    if qg == 3:
                        g = qt // 4
                        q0 = qt - 3
                        lastmm = None
                        for c in range(nC):
                            wt = [wtr_ev, ("act", a_e1_prev)] if c == 0 else []
                            if c == 0 and o_ev is not None:
                                wt.append(("dve", o_ev))
                            lastmm = R.pe(lambda e, c=c: e.matmul(
                                PA[:, 0:512], vT[:, c * 128:(c + 1) * 128],
                                wT[:, c * 512:(c + 1) * 512],
                                start=(c == 0), stop=(c == nC - 1)), wt)
                        omm_pe[(h, g)] = lastmm
                        bq = g % 2
                        wt = [("pe", lastmm)]
                        if ob_use[bq] is not None:
                            wt.append(("act", ob_use[bq]))
                        o_ev = R.dve(lambda e, bq=bq: e.tensor_copy(ob[bq][:, :], PA[:, 0:512]), wt)
                        R.dve(lambda e, bq=bq: e.tensor_tensor(sqb[:, :], ob[bq][:, :],
                                                               ob[bq][:, :], OP.mult))
                        R.dve(lambda e, q0=q0: e.tensor_tensor(acc[:, q0 * 128:q0 * 128 + 512],
                                                               acc[:, q0 * 128:q0 * 128 + 512],
                                                               sqb[:, :], OP.add))
                        wt = [("dve", o_ev)]
                        if obn_st[bq] is not None:
                            wt.append(obn_st[bq])
                        aon = R.act(lambda e, bq=bq, h=h: e.activation(
                            obn[bq][:, :], ob[bq][:, :], AF.Copy,
                            scale=nsf_cols[:, h:h + 1]), wt)
                        ob_use[bq] = aon
                        obn_st[bq] = R.store(oh_d[h * 128:(h + 1) * 128, q0 * 128:q0 * 128 + 512],
                                             obn[bq][:, :], [("act", aon)], ch=f"hon{bq}")
                    pend = None

                dw_prev = None
                for h in range(n_heads):
                    emit_wtr_and_omm()
                    wh = [("dve", head_gate)] if head_gate is not None else []
                    lq1 = R.load(q1[:, :], qs_d[h * 256:h * 256 + 128, :], wh, ch="hq1")
                    lq2 = R.load(q2[:, :], qs_d[h * 256 + 128:h * 256 + 256, :], wh, ch="hq2")
                    lk1 = R.load(k1[:, :], ks_d[h * 256:h * 256 + 128, :], wh, ch="hk1")
                    lk2 = R.load(k2[:, :], ks_d[h * 256 + 128:h * 256 + 256, :], wh, ch="hk2")
                    lv = R.load(vv[:, :], vs_d[h * 128:(h + 1) * 128, :], wh, ch="hv")
                    dvt = None
                    for c4 in range(0, nC, 4):
                        for k in range(4):
                            c = c4 + k
                            wt = []
                            if c == 0:
                                wt = [lv]
                                if wtr_ev is not None:
                                    wt.append(wtr_ev)
                                if a_e2_prev is not None:
                                    wt.append(("act", a_e2_prev))
                            pm = R.pe(lambda e, c=c: e.transpose(
                                PBv[:, c % 4, 0:128], vv[:, c * 128:(c + 1) * 128], idf[:, :]),
                                wt + ([("dve", dvt)] if k == 0 and dvt is not None else []))
                        dvt = R.dve(lambda e, c4=c4: e.tensor_copy(
                            vT[:, c4 * 128:(c4 + 4) * 128], PBv[:, 0:4, 0:128]), [("pe", pm)])
                    wtr_ev = ("dve", dvt)
                    for qt in range(nC):
                        qg = qt % 4
                        lastm1 = None
                        for tsl in range(nTs):
                            wt = []
                            if tsl == 0:
                                wt = [lq1, lk1]
                                if a_e1_prev is not None:
                                    wt.append(("act", a_e1_prev))
                                if o_ev is not None:
                                    wt.append(("dve", o_ev))
                            lastm1 = R.pe(lambda e, qt=qt, tsl=tsl: e.matmul(
                                PA[:, tsl * 512:(tsl + 1) * 512], q1[:, qt * 128:(qt + 1) * 128],
                                k1[:, tsl * 512:(tsl + 1) * 512], start=True, stop=True), wt)
                        lastm2 = None
                        for tsl in range(nTs):
                            wt = []
                            if tsl == 0:
                                wt = [lq2, lk2]
                                if wtr_ev is not None:
                                    wt.append(wtr_ev)
                                if a_e2_prev is not None:
                                    wt.append(("act", a_e2_prev))
                            lastm2 = R.pe(lambda e, qt=qt, tsl=tsl: e.matmul(
                                PB[:, tsl * 512:(tsl + 1) * 512], q2[:, qt * 128:(qt + 1) * 128],
                                k2[:, tsl * 512:(tsl + 1) * 512], start=True, stop=True), wt)
                        wt = [("pe", lastm1)]
                        if dw_prev is not None:
                            wt.append(("dve", dw_prev))
                        a_e1 = R.act(lambda e: e.activation(e1[:, :], PA[:, 0:S], AF.Exp, scale=SCALE32,
                                                            accum_out=rr[:, 0:1]), wt)
                        a_e1_prev = a_e1
                        a_e2 = R.act(lambda e: e.activation(e2[:, :], PB[:, 0:S], AF.Exp, scale=SCALE32,
                                                            accum_out=rr[:, 1:2]), [("pe", lastm2)])
                        a_e2_prev = a_e2
                        d = R.dve(lambda e: e.reciprocal(rr[:, 2:4], rr[:, 0:2]),
                                  [("act", a_e1), ("act", a_e2)])
                        d = R.dve(lambda e, h=h: e.tensor_scalar(rr[:, 3:4], rr[:, 3:4],
                                                                 lam_cols[:, h:h + 1], -1.0,
                                                                 OP.mult, OP.mult))
                        emit_wtr_and_omm()
                        ax1 = R.act(lambda e: e.activation(e1[:, :], e1[:, :], AF.Exp,
                                                           scale=rr[:, 2:3]), [("dve", d)])
                        ax2 = R.act(lambda e: e.activation(e2[:, :], e2[:, :], AF.Exp,
                                                           scale=rr[:, 3:4]))
                        wbuf = wbp[qt % 2]
                        wt = [("act", ax2)]
                        if wtr_pe is not None:
                            wt.append(("pe", wtr_pe))
                        dw = R.dve(lambda e, wbuf=wbuf: e.scalar_tensor_tensor(
                            wbuf[:, :], e1[:, :], 1.0, e2[:, :], OP.mult, OP.mult,
                            accum_out=rr[:, 4:5]), wt)
                        R.dve(lambda e: e.reciprocal(rr[:, 5:6], rr[:, 4:5]))
                        dw = R.dve(lambda e, wbuf=wbuf: e.tensor_scalar(wbuf[:, :], wbuf[:, :],
                                                                        rr[:, 5:6], None, OP.mult))
                        dw_prev = dw
                        pend = dict(h=h, qt=qt, qg=qg, dw=dw, wbuf=wbuf)
                    head_gate = R.cnt["dve"]
                emit_wtr_and_omm()
                head_gate = R.cnt["dve"]
            R.barrier()
            with ExitStack() as c2:
                with nc.psum_tensor("rsps", [128, 2048], F32) as pst:
                    dl = None
                    for c in range(nC):
                        wt = [("dve", dl)] if dl is not None and c >= 4 else []
                        pm = R.pe(lambda e, c=c: e.transpose(pst[:, (c % 4) * 512:(c % 4) * 512 + 128],
                                                             acc[:, c * 128:(c + 1) * 128], idf[:, :]), wt)
                        R.dve(lambda e, c=c: e.reduce_sum(tiny[:, 0:1],
                                                          pst[:, (c % 4) * 512:(c % 4) * 512 + 128],
                                                          axis=AX), [("pe", pm)])
                        aR = R.act(lambda e, c=c: e.activation(tiny2[:, 0:1], tiny[:, 0:1],
                                                               AF.Sqrt, bias=1e-6, scale=1.0 / D),
                                   [("dve", R.cnt["dve"])])
                        dl = R.dve(lambda e, c=c: e.reciprocal(rstd_cols[:, c:c + 1], tiny2[:, 0:1]),
                                   [("act", aR)])
            R.barrier()
            with ExitStack() as c2:
                sb2 = lambda n, sh, dt=F32: c2.enter_context(nc.sbuf_tensor(n, sh, dt))
                wos = c2.enter_context(nc.sbuf_tensor("bwos", [128, nD * D], F32))
                Ob = c2.enter_context(nc.sbuf_tensor("bOb", [128, nD * 512], F32))
                prj = [sb2(f"bpr{i}", [128, 512]) for i in range(2)]
                xtt = [sb2(f"bx{i}", [128, 512]) for i in range(2)]
                ott = [sb2(f"bt{i}", [128, 512]) for i in range(2)]
                PP = c2.enter_context(nc.psum_tensor("bpp", [128, 1024], F32))
                PT = c2.enter_context(nc.psum_tensor("bpt", [128, 1024], F32))
                lwo = None
                for fc in range(nD):
                    lwo = R.load(wos[:, fc * D:(fc + 1) * D], wo[fc * 128:(fc + 1) * 128, :], ch="bwo")
                ott_st = [None] * 4
                xtt_use = [None] * 4
                prj_use = [None, None]
                pt_use = [None, None]
                ob_ld_gate = None
                for qg in range(nTs):
                    lO = None
                    for fc in range(nD):
                        lO = R.load(Ob[:, fc * 512:(fc + 1) * 512],
                                    oh_d[fc * 128:(fc + 1) * 128, qg * 512:(qg + 1) * 512],
                                    [("pe", ob_ld_gate)] if ob_ld_gate is not None else [], ch="bob")
                    for d2c in range(nD):
                        b = d2c % 2
                        lastmm = None
                        for fc in range(nD):
                            wt = []
                            if fc == 0:
                                wt = [lO, lwo]
                                if prj_use[b] is not None:
                                    wt.append(("dve", prj_use[b]))
                            lastmm = R.pe(lambda e, fc=fc, d2c=d2c, b=b: e.matmul(
                                PP[:, b * 512:b * 512 + 512],
                                wos[:, fc * D + d2c * 128: fc * D + d2c * 128 + 128],
                                Ob[:, fc * 512:(fc + 1) * 512],
                                start=(fc == 0), stop=(fc == nD - 1)), wt)
                        if d2c == nD - 1:
                            ob_ld_gate = lastmm
                        dpr = R.dve(lambda e, b=b: e.tensor_copy(prj[b][:, :],
                                                                 PP[:, b * 512:b * 512 + 512]),
                                    [("pe", lastmm)])
                        prj_use[b] = dpr
                        tok0 = qg * 512
                        cx = R.load(xtt[b][:, :],
                                    src[tok0:tok0 + 512, d2c * 128:(d2c + 1) * 128]
                                    .rearrange("(a b) c -> b a c", b=128),
                                    [("dve", xtt_use[b])] if xtt_use[b] is not None else [],
                                    ch=f"bx{b}")
                        dub = None
                        for t4 in range(4):
                            c_tok = (tok0 + t4 * 128) // 128
                            sl = t4 % 2
                            pm = R.pe(lambda e, b=b, t4=t4, sl=sl: e.transpose(
                                PT[:, sl * 512:sl * 512 + 128],
                                prj[b][:, t4 * 128:(t4 + 1) * 128], idf[:, :]),
                                [("dve", dpr)] + ([("dve", pt_use[sl])] if pt_use[sl] is not None else []))
                            wt2 = [("pe", pm)]
                            if t4 == 0:
                                wt2.append(cx)
                                if ott_st[b] is not None:
                                    wt2.append(ott_st[b])
                            d1 = R.dve(lambda e, b=b, t4=t4, c_tok=c_tok, sl=sl: e.tensor_scalar(
                                ott[b][:, t4 * 128:(t4 + 1) * 128], PT[:, sl * 512:sl * 512 + 128],
                                rstd_cols[:, c_tok:c_tok + 1], None, OP.mult), wt2)
                            pt_use[sl] = d1
                            dub = R.dve(lambda e, b=b, t4=t4: e.tensor_tensor(
                                ott[b][:, t4 * 128:(t4 + 1) * 128],
                                ott[b][:, t4 * 128:(t4 + 1) * 128],
                                xtt[b][:, t4 * 128:(t4 + 1) * 128], OP.add))
                        xtt_use[b] = dub
                        ott_st[b] = R.store(dst[tok0:tok0 + 512, d2c * 128:(d2c + 1) * 128]
                                            .rearrange("(a b) c -> b a c", b=128),
                                            ott[b][:, :], [("dve", dub)], ch=f"bt{b}")
        R.barrier()

    ffn_phase(x_in, x1_d, True)
    attn_phase(x1_d, x2_d)
    ffn_phase(x2_d, y_out, False)
    R.op("t", lambda e: e.nop(),
         [(n, R.cnt[n]) for n in list(R.cnt) if n.startswith("st_") or n.startswith("ld_")])

    for n in list(R.cnt):
        sem[n] = ctx.enter_context(nc.semaphore("sem_" + n))
    blk = ctx.enter_context(nc.Block())
    last_wait = {}

    def replay(name):
        def go(e):
            for eng, fn, waits, inc, inc_n in R.steps:
                if eng != name:
                    continue
                for s, v in waits:
                    key = (name, s)
                    if last_wait.get(key, -1) < v:
                        e.wait_ge(sem[s], v)
                        last_wait[key] = v
                instr = fn(e)
                if inc is not None:
                    instr.then_inc(sem[inc], inc_n)
        return go

    blk.sync(replay("q"))
    blk.tensor(replay("t"))
    blk.vector(replay("v"))
    blk.scalar(replay("s"))
    blk.gpsimd(replay("g"))
    ctx.close()
    return nc


def kernel(L0_attn_x, L0_attn_Wq, L0_attn_Wk, L0_attn_Wv, L0_attn_Wo,
           L0_attn_norm_scale, L0_attn_lam, L0_ffn_w1, L0_ffn_w2):
    x = np.ascontiguousarray(np.asarray(L0_attn_x, dtype=np.float32))
    B = x.shape[0]
    nc = build_nc()
    common = {
        "w1": np.ascontiguousarray(L0_ffn_w1, dtype=np.float32),
        "w2": np.ascontiguousarray(L0_ffn_w2, dtype=np.float32),
        "wq": np.ascontiguousarray(L0_attn_Wq, dtype=np.float32),
        "wk": np.ascontiguousarray(L0_attn_Wk, dtype=np.float32),
        "wv": np.ascontiguousarray(L0_attn_Wv, dtype=np.float32),
        "wo": np.ascontiguousarray(L0_attn_Wo, dtype=np.float32),
        "ns": np.ascontiguousarray(L0_attn_norm_scale, dtype=np.float32),
        "lam": np.ascontiguousarray(L0_attn_lam, dtype=np.float32),
    }
    in_maps = [dict(common, x_in=np.ascontiguousarray(x[b])) for b in range(B)]
    res = run_bass_kernel_spmd(nc, in_maps, core_ids=list(range(B)))
    return np.stack([np.asarray(r["y_out"]) for r in res.results], axis=0)


# version 18 (final)
# speedup vs baseline: 1.0584x; 1.0584x over previous
import numpy as np
from contextlib import ExitStack

import concourse.bass as bass
import concourse.mybir as mybir
from concourse.bass_utils import run_bass_kernel_spmd
from concourse.alu_op_type import AluOpType as OP

F32 = mybir.dt.float32
BF16 = mybir.dt.bfloat16
AF = mybir.ActivationFunctionType
AX = mybir.AxisListType.X

ENGINES = ("q", "t", "v", "s", "g")
SCALE32 = float(np.float32(1.0) / np.sqrt(np.float32(128.0)))


class Rec:

    def __init__(self):
        self.steps = []
        from collections import defaultdict
        self.cnt = defaultdict(int)
        self.floor = {e: [] for e in ENGINES}

    def op(self, eng, fn, waits=(), inc=None, inc_n=1):
        w = list(self.floor[eng]) + [x for x in waits if x is not None and x[1] is not None]
        self.floor[eng] = []
        self.steps.append((eng, fn, w, inc, inc_n))
        if inc is not None:
            self.cnt[inc] += inc_n
            return self.cnt[inc]
        return None

    def load(self, out, in_, waits=(), ch="0"):
        sn = "ld_" + ch
        return (sn, self.op("q", lambda e: e.dma_start(out=out, in_=in_), waits, sn, 16))

    def store(self, out, in_, waits=(), ch="0"):
        sn = "st_" + ch
        return (sn, self.op("s", lambda e: e.dma_start(out=out, in_=in_), waits, sn, 16))

    def dma_sems(self, pfx):
        return [n for n in self.cnt if n.startswith(pfx)]

    def pe(self, fn, waits=()):
        return self.op("t", fn, waits, "pe", 1)

    def dve(self, fn, waits=()):
        w = list(waits)
        if self.cnt["dve"] > 0:
            w.append(("dve", self.cnt["dve"]))
        return self.op("v", fn, w, "dve", 1)

    def act(self, fn, waits=()):
        w = list(waits)
        if self.cnt["act"] > 0:
            w.append(("act", self.cnt["act"]))
        return self.op("s", fn, w, "act", 1)

    def pool(self, fn, waits=()):
        w = list(waits)
        if self.cnt["pool"] > 0:
            w.append(("pool", self.cnt["pool"]))
        return self.op("g", fn, w, "pool", 1)

    def barrier(self):
        lds = [(n, self.cnt[n]) for n in list(self.cnt) if n.startswith("ld_")]
        sts = [(n, self.cnt[n]) for n in list(self.cnt) if n.startswith("st_")]
        self.op("q", lambda e: e.nop(), lds + sts, "ph", 1)
        self.op("t", lambda e: e.drain(), [], "ph", 1)
        self.op("v", lambda e: e.drain(), [], "ph", 1)
        self.op("s", lambda e: e.drain(), sts, "ph", 1)
        self.op("g", lambda e: e.drain(), [], "ph", 1)
        tgt = self.cnt["ph"]
        for e in ENGINES:
            self.floor[e] = [("ph", tgt)]


def build_nc(S=2048, D=2048, F=8192, debug_taps=()):
    nC = S // 128
    nD = D // 128
    nH = F // 128
    nTs = S // 512
    n_heads = nD
    DQ = 2 * D
    TOKQ = 512
    nTokq = S // TOKQ
    D2G = min(4, nD)
    nD2g = nD // D2G

    nc = bass.Bass("TRN2", target_bir_lowering=False)

    def dram(name, shape, kind="Internal", dt=F32):
        k = "ExternalOutput" if name in debug_taps else kind
        return nc.dram_tensor(name, shape, dt, kind=k).ap()

    x_in = dram("x_in", [S, D], "ExternalInput")
    w1 = dram("w1", [F, D], "ExternalInput")
    w2 = dram("w2", [D, F], "ExternalInput")
    wq = dram("wq", [D, DQ], "ExternalInput")
    wk = dram("wk", [D, DQ], "ExternalInput")
    wv = dram("wv", [D, D], "ExternalInput")
    wo = dram("wo", [D, D], "ExternalInput")
    ns_in = dram("ns", [D], "ExternalInput")
    lam_in = dram("lam", [n_heads], "ExternalInput")
    y_out = dram("y_out", [S, D], "ExternalOutput")

    t1t_d = dram("t1t_d", [nH, 128, D], dt=BF16)
    t2t_d = dram("t2t_d", [nD2g, 128, D2G * nH * 128], dt=BF16)
    g_d = dram("g_d", [nH, 128, S])
    m_d = dram("m_d", [nTokq, nH, 128, TOKQ], dt=BF16)
    x1_d = dram("x1_d", [S, D])
    x2_d = dram("x2_d", [S, D])
    qs_d = dram("qs_d", [DQ, S])
    ks_d = dram("ks_d", [DQ, S])
    vs_d = dram("vs_d", [D, S])
    oh_d = dram("oh_d", [D, S])
    ntdbg_d = dram("ntdbg_d", [128, nD * S], dt=BF16)

    R = Rec()
    ctx = ExitStack()
    sem = {}

    sb = lambda n, sh, dt=F32: ctx.enter_context(nc.sbuf_tensor(n, sh, dt))
    idb = sb("idb", [128, 128], BF16)
    idf = sb("idf", [128, 128], F32)
    ones1 = sb("ones1", [1, 128], F32)
    ns_cols = sb("ns_cols", [128, nD], F32)
    nsf_cols = sb("nsf_cols", [128, nD], F32)
    lam_row = sb("lam_row", [1, n_heads], F32)
    lam_cols = sb("lam_cols", [128, n_heads], F32)
    facb = sb("facb", [128, 1], F32)
    s1_all = sb("s1_all", [128, nH], F32)
    s1g_all = sb("s1g_all", [128, nH], F32)
    s2_all = sb("s2_all", [128, nD], F32)
    isx_cols = sb("isx_cols", [128, nC], F32)
    ish_cols = sb("ish_cols", [128, nC], F32)
    rstd_cols = sb("rstd_cols", [128, nC], F32)
    tiny = sb("tiny", [128, 16], F32)
    tiny2 = sb("tiny2", [128, 16], F32)
    rowbuf = sb("rowbuf", [1, S], F32)
    cpool = sb("cpool", [128, 4], F32)
    trrow = sb("trrow", [128, 128], F32)
    _CONSTS = (1e-5, 1e-6, -128.0)
    for _i, _v in enumerate(_CONSTS):
        nc.const_aps.register(F32, _v, cpool[:, _i:_i + 1])

    for _i, _v in enumerate(_CONSTS):
        R.pool(lambda e, _i=_i, _v=_v: e.memset(cpool[:, _i:_i + 1], _v))
    c_ns = ("ld_cns", R.op("q", lambda e: e.dma_start(out=ns_cols[:, :],
                                           in_=ns_in.rearrange("(a b) -> b a", b=128),
                                           allow_slow_non_contiguous=True), [], "ld_cns", 16))
    c_lam = R.load(lam_row[0:1, :], lam_in[:], ch="clam")
    R.pool(lambda e: e.memset(ones1[:, :], 1.0))
    R.pool(lambda e: e.memset(idb[:, :], 0.0))
    R.pool(lambda e: e.affine_select(out=idb[:, :], in_=idb[:, :], compare_op=OP.not_equal,
                                     fill=1.0, base=0, pattern=[[-1, 128]], channel_multiplier=1))
    R.pool(lambda e: e.memset(idf[:, :], 0.0))
    p_id = R.pool(lambda e: e.affine_select(out=idf[:, :], in_=idf[:, :], compare_op=OP.not_equal,
                                            fill=1.0, base=0, pattern=[[-1, 128]], channel_multiplier=1))
    R.dve(lambda e: e.memset(trrow[:, :], 0.0))
    d_ml = R.dve(lambda e: e.reduce_sum(tiny[0:1, 0:1], lam_row[0:1, :], axis=AX), [c_lam])
    d_ml = R.dve(lambda e: e.tensor_scalar(tiny[0:1, 0:1], tiny[0:1, 0:1], -1.0 / n_heads, 1.0,
                                           OP.mult, OP.add))
    with nc.psum_tensor("ps_init", [128, 16], F32) as psi:
        p_b = R.pe(lambda e: e.matmul(psi[:, 0:1], ones1[0:1, 0:128], tiny[0:1, 0:1],
                                      start=True, stop=True), [("pool", p_id), ("dve", d_ml)])
        d_f = R.dve(lambda e: e.tensor_copy(facb[:, :], psi[:, 0:1]), [("pe", p_b)])
        p_b2 = R.pe(lambda e: e.matmul(psi[:, 0:n_heads], ones1[0:1, 0:128], lam_row[0:1, :],
                                       start=True, stop=True), [("dve", d_f)])
        d_lc = R.dve(lambda e: e.tensor_copy(lam_cols[:, :], psi[:, 0:n_heads]), [("pe", p_b2)])
        R.dve(lambda e: e.tensor_scalar(nsf_cols[:, :], ns_cols[:, :], facb[:, 0:1], None, OP.mult),
              [c_ns])
    R.barrier()

    def ladder_pair(bufs):
        offs = 0.0
        last_act = [None] * len(bufs)
        last_dve = [b["z_ready"] for b in bufs]
        for k in (128.0, 64.0, 32.0, 16.0, 8.0, 4.0, 2.0, 1.0):
            bias = -(k + offs)
            for bi, b in enumerate(bufs):
                u, sg = b["u"], b["sgn"]
                last_act[bi] = R.act(
                    lambda e, u=u, sg=sg, bias=bias: e.activation(sg, u, AF.Sign, bias=bias, scale=1.0),
                    [("dve", last_dve[bi])])
            for bi, b in enumerate(bufs):
                u, sg = b["u"], b["sgn"]
                last_dve[bi] = R.dve(
                    lambda e, u=u, sg=sg, k=k: e.scalar_tensor_tensor(u, sg, -k / 2.0, u, OP.mult, OP.add),
                    [("act", last_act[bi])])
            offs += k / 2.0
        outs = []
        for bi, b in enumerate(bufs):
            u, zk, m = b["u"], b["zkeep"], b["out_m"]
            zoff = b.get("zoff", 0.0)
            R.dve(lambda e, u=u, zk=zk: e.scalar_tensor_tensor(u, u, -1.0, zk, OP.mult, OP.add))
            d = R.dve(lambda e, u=u, m=m, zoff=zoff: e.tensor_scalar(m, u, 1.0, zoff - 0.5,
                                                                     OP.mult, OP.add))
            outs.append(d)
        return outs

    def layernorm_chunk(xt, xn, sq, mean, var, rstd, x_ready):
        R.dve(lambda e: e.reduce_sum(mean, xt, axis=AX), [x_ready])
        d1 = R.dve(lambda e: e.tensor_scalar(mean, mean, 1.0 / D, None, OP.mult))
        a1 = R.act(lambda e: e.activation(sq, xt, AF.Square, accum_out=var), [x_ready])
        R.dve(lambda e: e.tensor_scalar(var, var, 1.0 / D, None, OP.mult), [("act", a1)])
        R.dve(lambda e: e.tensor_tensor(sq[:, 0:1], mean, mean, OP.mult))
        d2 = R.dve(lambda e: e.tensor_tensor(var, var, sq[:, 0:1], OP.subtract))
        a2 = R.act(lambda e: e.activation(rstd, var, AF.Sqrt, bias=1e-5, scale=1.0), [("dve", d2)])
        R.dve(lambda e: e.reciprocal(rstd, rstd), [("act", a2)])
        dn = R.dve(lambda e: e.tensor_scalar(xn, xt, mean, rstd, OP.subtract, OP.mult))
        return dn

    def ffn_phase(src, dst, first):
        if first:
            ternarize()
        with ExitStack() as cph:
            pfx = "A" if first else "B"
            bsh = cph.enter_context(nc.sbuf_tensor(pfx + "bsh", [128, S], F32))
            with ExitStack() as ca:
                nT = ca.enter_context(nc.sbuf_tensor(pfx + "nT", [128, nD * S], BF16))
                bisx = ca.enter_context(nc.sbuf_tensor(pfx + "bisx", [128, S], F32))
                ffn_ln_quant(src, nT, bisx, ca, pfx)
                if "ntdbg_d" in debug_taps and first:
                    R.store(ntdbg_d[:, :], nT[:, :], [("dve", R.cnt["dve"])], ch="dbg")
                R.barrier()
                ffn_mm1(nT, bisx, bsh, ca, pfx)
                R.barrier()
            ffn_mm2(src, dst, bsh, cph, pfx)
        R.barrier()

    def ternarize():
        with ExitStack() as c2:
            sb2 = lambda n, sh, dt=F32: c2.enter_context(nc.sbuf_tensor(n, sh, dt))
            wb = [sb2(f"p0w{i}", [128, D]) for i in range(2)]
            cb = sb2("p0c", [128, D], BF16)
            cb2 = sb2("p0c2", [128, D], BF16)
            tb = sb2("p0t", [128, D], BF16)
            tsp = [sb2(f"p0sp{i}", [128, D], BF16) for i in range(2)]
            shf = sb2("p0shf", [128, 2])
            with nc.psum_tensor("p0ps", [128, 4096], BF16) as pst:
                used = [None, None]
                spilled = [None, None]
                slot_ev = [None] * 4
                for r in range(nH):
                    i = r % 2
                    cw = R.load(wb[i][:, :], w1[r * 128:(r + 1) * 128, :],
                                [("dve", used[i])] if used[i] is not None else [], ch=f"p0w{i}")
                    R.dve(lambda e, i=i: e.reduce_sum(shf[:, 0:1], wb[i][:, :], axis=AX,
                                                      apply_absolute_value=True), [cw])
                    R.dve(lambda e, r=r: e.tensor_scalar(s1_all[:, r:r + 1], shf[:, 0:1], 1.0 / D, 1e-5,
                                                         OP.mult, OP.max))
                    d_s = R.dve(lambda e, r=r: e.tensor_scalar(shf[:, 0:1], s1_all[:, r:r + 1], 0.5, None, OP.mult))
                    d_s = R.dve(lambda e: e.tensor_scalar(shf[:, 1:2], shf[:, 0:1], -1.0, None, OP.mult))
                    aA = R.act(lambda e, i=i: e.activation(cb[:, :], wb[i][:, :], AF.Sign,
                                                           bias=shf[:, 1:2], scale=1.0), [cw, ("dve", d_s)])
                    aB = R.act(lambda e, i=i: e.activation(cb2[:, :], wb[i][:, :], AF.Sign,
                                                           bias=shf[:, 0:1], scale=1.0), [cw])
                    dt_ = R.dve(lambda e: e.tensor_tensor(tb[:, :], cb[:, :], cb2[:, :], OP.add),
                                [("act", aB)])
                    R.dve(lambda e, r=r: e.tensor_scalar(s1g_all[:, r:r + 1], s1_all[:, r:r + 1],
                                                         0.5, None, OP.mult))
                    used[i] = dt_
                    j = r % 2
                    last = None
                    pstv = pst.rearrange("p (c x) -> p c x", x=1024)
                    tspv = tsp[j].rearrange("p (c x) -> p c x", x=128)
                    for d4 in range(0, nD, 4):
                        ng4 = min(4, nD - d4)
                        for k in range(ng4):
                            dch = d4 + k
                            wt = [("dve", dt_)] if dch == 0 else []
                            if k == 0 and last is not None:
                                wt.append(("dve", last))
                            if dch == 0 and spilled[j] is not None:
                                wt.append(spilled[j])
                            pt = R.pe(lambda e, k=k, dch=dch: e.matmul(
                                pstv[:, k, 0:128],
                                tb[:, dch * 128:(dch + 1) * 128], idb[:, :],
                                is_transpose=True, start=True, stop=True), wt)
                        last = R.dve(lambda e, d4=d4, ng4=ng4, tspv=tspv: e.tensor_copy(
                            tspv[:, d4:d4 + ng4, 0:128], pstv[:, 0:ng4, 0:128]), [("pe", pt)])
                    spilled[j] = R.store(t1t_d[r, :, :], tsp[j][:, :], [("dve", last)], ch=f"p0sp{j}")
        R.barrier()
        with ExitStack() as c2:
            sb2 = lambda n, sh, dt=F32: c2.enter_context(nc.sbuf_tensor(n, sh, dt))
            wb_w2 = [sb2(f"p1w{i}", [128, F]) for i in range(2)]
            cb_w2 = sb2("p1c", [128, F], BF16)
            cb2_w2 = sb2("p1c2", [128, F], BF16)
            tb_w2 = sb2("p1t", [128, F], BF16)
            tsp_w2 = [sb2("p1sp0", [128, nH * 128], BF16)]
            shf_w2 = sb2("p1shf", [128, 2])
            with nc.psum_tensor("p1ps", [128, 4096], BF16) as pst_w2:
                used_w2 = [None, None]
                spilled_w2 = [None]
                slot_ev2 = [None] * 4
                for r in range(nD):
                    i = r % 2
                    cw_w2 = R.load(wb_w2[i][:, :], w2[r * 128:(r + 1) * 128, :],
                                [("dve", used_w2[i])] if used_w2[i] is not None else [], ch=f"p1w{i}")
                    R.dve(lambda e, i=i: e.reduce_sum(shf_w2[:, 0:1], wb_w2[i][:, :], axis=AX,
                                                      apply_absolute_value=True), [cw_w2])
                    R.dve(lambda e, r=r: e.tensor_scalar(s2_all[:, r:r + 1], shf_w2[:, 0:1], 1.0 / F, 1e-5,
                                                         OP.mult, OP.max))
                    d_s = R.dve(lambda e, r=r: e.tensor_scalar(shf_w2[:, 0:1], s2_all[:, r:r + 1], 0.5, None, OP.mult))
                    d_s = R.dve(lambda e: e.tensor_scalar(shf_w2[:, 1:2], shf_w2[:, 0:1], -1.0, None, OP.mult))
                    aA = R.act(lambda e, i=i: e.activation(cb_w2[:, :], wb_w2[i][:, :], AF.Sign,
                                                           bias=shf_w2[:, 1:2], scale=1.0), [cw_w2, ("dve", d_s)])
                    aB = R.act(lambda e, i=i: e.activation(cb2_w2[:, :], wb_w2[i][:, :], AF.Sign,
                                                           bias=shf_w2[:, 0:1], scale=1.0), [cw_w2])
                    dt__w2 = R.dve(lambda e: e.tensor_tensor(tb_w2[:, :], cb_w2[:, :], cb2_w2[:, :], OP.add),
                                   [("act", aB)])
                    used_w2[i] = dt__w2
                    j = 0
                    qtr, jj = r // D2G, r % D2G
                    last_w2 = None
                    pstv2 = pst_w2.rearrange("p (c x) -> p c x", x=1024)
                    tspv2 = tsp_w2[j].rearrange("p (c x) -> p c x", x=128)
                    for h4 in range(0, nH, 4):
                        ng4 = min(4, nH - h4)
                        for k in range(ng4):
                            hc = h4 + k
                            wt_w2 = [("dve", dt__w2)] if hc == 0 else []
                            if k == 0 and last_w2 is not None:
                                wt_w2.append(("dve", last_w2))
                            if hc == 0 and spilled_w2[j] is not None:
                                wt_w2.append(spilled_w2[j])
                            pt_w2 = R.pe(lambda e, k=k, hc=hc: e.matmul(
                                pstv2[:, k, 0:128],
                                tb_w2[:, hc * 128:(hc + 1) * 128], idb[:, :],
                                is_transpose=True, start=True, stop=True), wt_w2)
                        last_w2 = R.dve(lambda e, h4=h4, ng4=ng4, tspv2=tspv2: e.tensor_copy(
                            tspv2[:, h4:h4 + ng4, 0:128], pstv2[:, 0:ng4, 0:128]), [("pe", pt_w2)])
                    spilled_w2[j] = R.store(t2t_d[qtr, :, jj * nH * 128:(jj + 1) * nH * 128],
                                         tsp_w2[j][:, :], [("dve", last_w2)], ch=f"p1sp{j}")
        R.barrier()

    def ffn_ln_quant(src, nT, bisx, cph, pfx):
        with ExitStack() as c2:
            sb2 = lambda n, sh, dt=F32: c2.enter_context(nc.sbuf_tensor(pfx + n, sh, dt))
            xt = [sb2(f"lx{i}", [128, D]) for i in range(2)]
            xn = [sb2(f"ln{i}", [128, D]) for i in range(2)]
            sq = xn
            zk = [sb2(f"lz{i}", [128, D]) for i in range(2)]
            sg = [sb2(f"lg{i}", [128, D]) for i in range(2)]
            nb = [sb2(f"lb{i}", [128, D], BF16) for i in range(2)]
            mv = sb2("lmv", [128, 8])
            with nc.psum_tensor(pfx + "lps", [128, 4096], BF16) as pst:
                used = [None, None]
                slot_ev = [None] * 4
                for c0 in range(0, nC, 2):
                    pair = [c for c in (c0, c0 + 1) if c < nC]
                    bufs = []
                    for c in pair:
                        i = c % 2
                        cx = R.load(xt[i][:, :], src[c * 128:(c + 1) * 128, :],
                                    [("dve", used[i])] if used[i] is not None else [], ch=f"lnx{i}")
                        dn = layernorm_chunk(xt[i][:, :], xn[i][:, :], sq[i][:, :],
                                             mv[:, 4 * i:4 * i + 1], mv[:, 4 * i + 1:4 * i + 2],
                                             mv[:, 4 * i + 2:4 * i + 3], cx)
                        q = mv[:, 4 * i + 3:4 * i + 4]
                        R.dve(lambda e, i=i, q=q: e.reduce_max(q, xn[i][:, :], axis=AX,
                                                               apply_absolute_value=True))
                        R.dve(lambda e, q=q: e.tensor_scalar(q, q, 1e-5, None, OP.max))
                        R.dve(lambda e, q=q, c=c: e.tensor_scalar(isx_cols[:, c:c + 1], q,
                                                                  1.0 / 127.0, None, OP.mult))
                        R.dve(lambda e, q=q: e.reciprocal(q, q))
                        R.dve(lambda e, q=q: e.tensor_scalar(q, q, 127.0, None, OP.mult))
                        R.dve(lambda e, i=i, q=q: e.tensor_scalar(zk[i][:, :], xn[i][:, :], q, 128.5,
                                                                  OP.mult, OP.add))
                        dz = R.dve(lambda e, i=i: e.tensor_copy(xn[i][:, :], zk[i][:, :]))
                        bufs.append(dict(u=xn[i][:, :], sgn=sg[i][:, :], zkeep=zk[i][:, :],
                                         out_m=nb[i][:, :], z_ready=dz, c=c, i=i))
                    outs = ladder_pair(bufs)
                    for b, o in zip(bufs, outs):
                        c, i = b["c"], b["i"]
                        last = None
                        for dch in range(nD):
                            sl = dch % 4
                            wt = [("dve", o)] if dch == 0 else []
                            if slot_ev[sl] is not None:
                                wt.append(("dve", slot_ev[sl]))
                            pt = R.pe(lambda e, i=i, sl=sl, dch=dch: e.matmul(
                                pst[:, sl * 1024:sl * 1024 + 128],
                                nb[i][:, dch * 128:(dch + 1) * 128], idb[:, :],
                                is_transpose=True, start=True, stop=True), wt)
                            last = R.dve(lambda e, sl=sl, dch=dch, c=c: e.tensor_copy(
                                nT[:, dch * S + c * 128: dch * S + c * 128 + 128],
                                pst[:, sl * 1024:sl * 1024 + 128]), [("pe", pt)])
                            slot_ev[sl] = last
                        used[i] = last
            dv = R.cnt["dve"]
            with nc.psum_tensor(pfx + "lpb", [128, 512], F32) as pb:
                dcp = R.dve(lambda e: e.tensor_copy(trrow[:, 0:nC], isx_cols[:, 0:nC]), [("dve", dv)])
                ptr = R.pe(lambda e: e.transpose(pb[:, 0:128], trrow[:, 0:128], idf[:, :]),
                           [("dve", dcp)])
                dtr = R.dve(lambda e: e.tensor_copy(trrow[:, 0:128], pb[:, 0:128]), [("pe", ptr)])
                cr = R.load(rowbuf[0:1, 0:S], trrow[0:nC, 0:128], [("dve", dtr)], ch="row")
                dl = None
                for tsl in range(nTs):
                    pbm = R.pe(lambda e, tsl=tsl: e.matmul(pb[:, 0:512],
                                                           ones1[0:1, 0:128],
                                                           rowbuf[0:1, tsl * 512:(tsl + 1) * 512],
                                                           start=True, stop=True),
                               [cr, ("dve", dtr)] + ([("dve", dl)] if dl is not None else []))
                    dl = R.dve(lambda e, tsl=tsl: e.tensor_copy(bisx[:, tsl * 512:(tsl + 1) * 512],
                                                                pb[:, 0:512]), [("pe", pbm)])

    def ffn_mm1(nT, bisx, bsh, cph, pfx):
        with ExitStack() as c2:
            sb2 = lambda n, sh, dt=F32: c2.enter_context(nc.sbuf_tensor(pfx + n, sh, dt))
            t1s = [sb2(f"mw{i}", [128, nD * 128], BF16) for i in range(2)]
            y2b = [sb2(f"my{i}", [128, S]) for i in range(2)]
            gb = [sb2(f"mg{i}", [128, S]) for i in range(2)]
            amx = sb2("mam", [128, S])
            amn = sb2("mam2", [128, S])
            psy = [c2.enter_context(nc.psum_tensor(pfx + f"mp{i}", [128, 2048], F32)) for i in range(2)]
            R.dve(lambda e: e.memset(amx[:, :], 0.0))
            R.dve(lambda e: e.memset(amn[:, :], 0.0))
            usedw = [None, None]
            evacd = [None, None]
            gf = [None, None]
            gst = [None, None]
            for hc in range(nH):
                i = hc % 2
                cw = R.load(t1s[i][:, :], t1t_d[hc, :, :],
                            [("pe", usedw[i])] if usedw[i] is not None else [], ch=f"m1w{i}")
                lastmm = None
                dlast = None
                for tsl in range(nTs):
                    for dch in range(nD):
                        wt = []
                        if dch == 0 and tsl == 0:
                            wt = [cw]
                            if evacd[i] is not None:
                                wt.append(("dve", evacd[i]))
                        lastmm = R.pe(lambda e, i=i, dch=dch, tsl=tsl: e.matmul(
                            psy[i][:, tsl * 512:(tsl + 1) * 512],
                            t1s[i][:, dch * 128:(dch + 1) * 128],
                            nT[:, dch * S + tsl * 512: dch * S + (tsl + 1) * 512],
                            start=(dch == 0), stop=(dch == nD - 1)), wt)
                    wt = [("pe", lastmm)]
                    if tsl == 0 and gf[i] is not None:
                        wt.append(("act", gf[i]))
                    dlast = R.dve(lambda e, i=i, tsl=tsl: e.tensor_tensor(
                        y2b[i][:, tsl * 512:(tsl + 1) * 512], psy[i][:, tsl * 512:(tsl + 1) * 512],
                        bisx[:, tsl * 512:(tsl + 1) * 512], OP.mult), wt)
                usedw[i] = lastmm
                evacd[i] = dlast
                wg = [("dve", dlast)]
                if gst[i] is not None:
                    wg.append(gst[i])
                ag = R.act(lambda e, i=i, hc=hc: e.activation(gb[i][:, :], y2b[i][:, :],
                                                              AF.Gelu_apprx_tanh,
                                                              scale=s1g_all[:, hc:hc + 1]), wg)
                gf[i] = ag
                R.dve(lambda e, i=i: e.tensor_tensor(amx[:, :], gb[i][:, :], amx[:, :], OP.max),
                      [("act", ag)])
                R.dve(lambda e, i=i: e.tensor_tensor(amn[:, :], gb[i][:, :], amn[:, :], OP.min))
                gst[i] = R.store(g_d[hc, :, :], gb[i][:, :], [("act", ag)], ch=f"m1g{i}")
            pt0 = psy[0]
            pt1 = psy[1]
            R.dve(lambda e: e.tensor_scalar(amn[:, :], amn[:, :], -1.0, None, OP.mult))
            R.dve(lambda e: e.tensor_tensor(amx[:, :], amx[:, :], amn[:, :], OP.max))
            dve_amax = R.cnt["dve"]
            dl = None
            for c in range(nC):
                wt = [("dve", dve_amax)] if c == 0 else []
                if dl is not None and c >= 4:
                    wt.append(("dve", dl))
                pm = R.pe(lambda e, c=c: e.transpose(pt0[:, (c % 4) * 512:(c % 4) * 512 + 128],
                                                     amx[:, c * 128:(c + 1) * 128], idf[:, :]), wt)
                R.dve(lambda e, c=c: e.reduce_max(tiny[:, 0:1],
                                                  pt0[:, (c % 4) * 512:(c % 4) * 512 + 128],
                                                  axis=AX), [("pe", pm)])
                R.dve(lambda e: e.tensor_scalar(tiny[:, 0:1], tiny[:, 0:1], 1e-5, None, OP.max))
                R.dve(lambda e, c=c: e.tensor_scalar(ish_cols[:, c:c + 1], tiny[:, 0:1],
                                                     1.0 / 127.0, None, OP.mult))
                R.dve(lambda e: e.reciprocal(tiny2[:, 0:1], tiny[:, 0:1]))
                dl = R.dve(lambda e, c=c: e.tensor_scalar(rstd_cols[:, c:c + 1], tiny2[:, 0:1],
                                                          127.0, None, OP.mult))
            dcp = R.dve(lambda e: e.tensor_copy(trrow[:, 0:nC], rstd_cols[:, 0:nC]), [("dve", dl)])
            ptr = R.pe(lambda e: e.transpose(pt1[:, 0:128], trrow[:, 0:128], idf[:, :]),
                       [("dve", dcp)])
            dtr = R.dve(lambda e: e.tensor_copy(trrow[:, 0:128], pt1[:, 0:128]), [("pe", ptr)])
            cr = R.load(rowbuf[0:1, 0:S], trrow[0:nC, 0:128], [("dve", dtr)], ch="row")
            dl = None
            for tsl in range(nTs):
                pbm = R.pe(lambda e, tsl=tsl: e.matmul(pt1[:, 0:512], ones1[0:1, 0:128],
                                                       rowbuf[0:1, tsl * 512:(tsl + 1) * 512],
                                                       start=True, stop=True),
                           [cr, ("dve", dtr)] + ([("dve", dl)] if dl is not None else []))
                dl = R.dve(lambda e, tsl=tsl: e.tensor_copy(bsh[:, tsl * 512:(tsl + 1) * 512],
                                                            pt1[:, 0:512]), [("pe", pbm)])

    def ffn_mm2(src, dst, bsh, cph, pfx):
        NG = 4
        ngrp = nH // NG
        with ExitStack() as c2:
            sb2 = lambda n, sh, dt=F32: c2.enter_context(nc.sbuf_tensor(pfx + n, sh, dt))
            t2s = c2.enter_context(nc.sbuf_tensor(pfx + "2w", [128, D2G * nH * 128], BF16))
            mbuf = c2.enter_context(nc.sbuf_tensor(pfx + "2m", [128, nH * TOKQ], BF16))
            gld = [sb2(f"2g{i}", [128, NG * TOKQ]) for i in range(2)]
            zkb = [sb2(f"2z{i}", [128, NG * TOKQ]) for i in range(2)]
            sgb = [sb2(f"2s{i}", [128, NG * TOKQ], BF16) for i in range(2)]
            bsh4 = sb2("2b4", [128, NG * TOKQ])
            o2f = [sb2(f"2o{i}", [128, 512]) for i in range(2)]
            xtt = [sb2(f"2x{i}", [128, 512]) for i in range(2)]
            ott = [sb2(f"2t{i}", [128, 512]) for i in range(2)]
            psm = c2.enter_context(nc.psum_tensor(pfx + "2p", [128, D2G * 512], F32))
            pstr = c2.enter_context(nc.psum_tensor(pfx + "2pt", [128, 1024], F32))
            mg_ready = {}
            pe_at = {}
            st_grp = {}
            ott_st = [None] * 4
            xtt_use = [None] * 4
            o2f_use = [None, None]
            tr_ev = [None, None]
            ptr_use = [None, None]
            t2_used = None
            for qtr in range(nD2g):
                cw = R.load(t2s[:, :], t2t_d[qtr, :, :],
                            [("pe", t2_used)] if t2_used is not None else [], ch="m2w")
                for tq in range(nTokq):
                    if qtr == 0:
                        db4 = None
                        for k in range(NG):
                            db4 = R.dve(lambda e, k=k, tq=tq: e.tensor_copy(
                                bsh4[:, k * TOKQ:(k + 1) * TOKQ],
                                bsh[:, tq * TOKQ:(tq + 1) * TOKQ]))
                        lastg = [None, None]
                        for g0 in range(0, ngrp, 2):
                            prs = []
                            for grp in (g0, g0 + 1):
                                if grp >= ngrp:
                                    continue
                                i = grp % 2
                                wt0 = [("dve", lastg[i])] if lastg[i] is not None else []
                                if tq > 0:
                                    wt0.append(("pe", pe_at[(tq - 1, grp)]))
                                    if (tq - 1, grp) in st_grp:
                                        wt0.append(st_grp[(tq - 1, grp)])
                                cg = None
                                for k in range(NG):
                                    hc = grp * NG + k
                                    cg = R.load(gld[i][:, k * TOKQ:(k + 1) * TOKQ],
                                                g_d[hc, :, tq * TOKQ:(tq + 1) * TOKQ],
                                                wt0 if k == 0 else [], ch=f"m2g{i}")
                                R.dve(lambda e, i=i: e.tensor_tensor(
                                    zkb[i][:, :], gld[i][:, :], bsh4[:, :], OP.mult),
                                    [cg, ("dve", db4)])
                                dz = R.dve(lambda e, i=i: e.tensor_scalar(gld[i][:, :], zkb[i][:, :],
                                                                          1.0, 128.5, OP.mult, OP.add))
                                prs.append(dict(u=gld[i][:, :], sgn=sgb[i][:, :], zkeep=zkb[i][:, :],
                                                out_m=mbuf[:, grp * NG * TOKQ:(grp + 1) * NG * TOKQ],
                                                z_ready=dz, grp=grp, i=i, zoff=128.5))
                            outs = ladder_pair(prs)
                            for pr, o in zip(prs, outs):
                                lastg[pr["i"]] = o
                                grp = pr["grp"]
                                mg_ready[(tq, grp)] = ("dve", o)
                                st_grp[(tq, grp)] = R.store(
                                    m_d[tq, grp * NG:(grp + 1) * NG, :, :],
                                    mbuf[:, grp * NG * TOKQ:(grp + 1) * NG * TOKQ],
                                    [("dve", o)], ch=f"m2m{grp}")
                    else:
                        for grp in range(ngrp):
                            prev_tq = tq - 1 if tq > 0 else nTokq - 1
                            wtl = [("pe", pe_at[(prev_tq, grp)])]
                            if qtr == 1:
                                wtl.append(st_grp[(nTokq - 1, grp)])
                            cm = R.load(mbuf[:, grp * NG * TOKQ:(grp + 1) * NG * TOKQ],
                                        m_d[tq, grp * NG:(grp + 1) * NG, :, :],
                                        wtl, ch=f"m2mr{grp}")
                            mg_ready[(tq, grp)] = cm
                    lastmm = None
                    for hc in range(nH):
                        for j in range(D2G):
                            wt = []
                            if hc % NG == 0 and j == 0:
                                wt = [mg_ready[(tq, hc // NG)]]
                                if hc == 0:
                                    wt.append(cw)
                                    if o2f_use[0] is not None:
                                        wt.append(("act", o2f_use[0]))
                                    if o2f_use[1] is not None:
                                        wt.append(("act", o2f_use[1]))
                            lastmm = R.pe(lambda e, hc=hc, j=j: e.matmul(
                                psm[:, j * 512:j * 512 + 512],
                                t2s[:, (j * nH + hc) * 128:(j * nH + hc) * 128 + 128],
                                mbuf[:, hc * TOKQ: hc * TOKQ + 512],
                                start=(hc == 0), stop=(hc == nH - 1)), wt)
                            if hc % NG == NG - 1 and j == D2G - 1:
                                pe_at[(tq, hc // NG)] = lastmm
                    t2_used = lastmm
                    for j in range(D2G):
                        d2c = qtr * D2G + j
                        ib = j % 2
                        tok0 = tq * TOKQ
                        wt = [("pe", lastmm)]
                        if tr_ev[ib] is not None:
                            wt.append(("dve", tr_ev[ib]))
                        a1 = R.act(lambda e, j=j, d2c=d2c, ib=ib: e.activation(
                            o2f[ib][:, :], psm[:, j * 512:j * 512 + 512], AF.Copy,
                            scale=s2_all[:, d2c:d2c + 1]), wt)
                        o2f_use[ib] = a1
                        cx = R.load(xtt[ib][:, :],
                                    src[tok0:tok0 + 512, d2c * 128:(d2c + 1) * 128]
                                    .rearrange("(a b) c -> b a c", b=128),
                                    [("dve", xtt_use[ib])] if xtt_use[ib] is not None else [],
                                    ch=f"m2x{ib}")
                        dub = None
                        for t4 in range(4):
                            c_tok = (tok0 + t4 * 128) // 128
                            sl = t4 % 2
                            pm = R.pe(lambda e, t4=t4, ib=ib, sl=sl: e.transpose(
                                pstr[:, sl * 512:sl * 512 + 128],
                                o2f[ib][:, t4 * 128:(t4 + 1) * 128], idf[:, :]),
                                [("act", a1)] + ([("dve", ptr_use[sl])] if ptr_use[sl] is not None else []))
                            wt2 = [("pe", pm)]
                            if t4 == 0:
                                wt2.append(cx)
                                if ott_st[ib] is not None:
                                    wt2.append(ott_st[ib])
                            d1 = R.dve(lambda e, t4=t4, ib=ib, c_tok=c_tok, sl=sl: e.tensor_scalar(
                                ott[ib][:, t4 * 128:(t4 + 1) * 128], pstr[:, sl * 512:sl * 512 + 128],
                                ish_cols[:, c_tok:c_tok + 1], 0.25, OP.mult, OP.mult), wt2)
                            ptr_use[sl] = d1
                            dub = R.dve(lambda e, t4=t4, ib=ib: e.tensor_tensor(
                                ott[ib][:, t4 * 128:(t4 + 1) * 128],
                                ott[ib][:, t4 * 128:(t4 + 1) * 128],
                                xtt[ib][:, t4 * 128:(t4 + 1) * 128], OP.add))
                        xtt_use[ib] = dub
                        tr_ev[ib] = dub
                        ott_st[ib] = R.store(dst[tok0:tok0 + 512, d2c * 128:(d2c + 1) * 128]
                                             .rearrange("(a b) c -> b a c", b=128),
                                             ott[ib][:, :], [("dve", dub)], ch=f"m2t{ib}")

    def attn_phase(src, dst):
        with ExitStack() as cph:
          with ExitStack() as cxn:
            xnTh = cxn.enter_context(nc.sbuf_tensor("at_xnTh", [128, nD * S], BF16))
            xnTl = cxn.enter_context(nc.sbuf_tensor("at_xnTl", [128, nD * S], BF16))
            with ExitStack() as c2:
                sb2 = lambda n, sh, dt=F32: c2.enter_context(nc.sbuf_tensor(n, sh, dt))
                xt = [sb2(f"a2x{i}", [128, D]) for i in range(2)]
                xn = [sb2(f"a2n{i}", [128, D]) for i in range(2)]
                sq = xn
                mv = sb2("a2mv", [128, 8])
                with nc.psum_tensor("a2ps", [128, 2048], F32) as pst:
                    used = [None, None]
                    slot_ev = [None] * 4
                    for c in range(nC):
                        i = c % 2
                        cx = R.load(xt[i][:, :], src[c * 128:(c + 1) * 128, :],
                                    [("dve", used[i])] if used[i] is not None else [], ch=f"a2x{i}")
                        dn = layernorm_chunk(xt[i][:, :], xn[i][:, :], sq[i][:, :],
                                             mv[:, 4 * i:4 * i + 1], mv[:, 4 * i + 1:4 * i + 2],
                                             mv[:, 4 * i + 2:4 * i + 3], cx)
                        dl = None
                        for dch in range(nD):
                            sl = dch % 4
                            wt = [("dve", dn)] if dch == 0 else []
                            if slot_ev[sl] is not None:
                                wt.append(("dve", slot_ev[sl]))
                            pm = R.pe(lambda e, i=i, sl=sl, dch=dch: e.transpose(
                                pst[:, sl * 512:sl * 512 + 128],
                                xn[i][:, dch * 128:(dch + 1) * 128], idf[:, :]), wt)
                            R.dve(lambda e, sl=sl, dch=dch, c=c: e.tensor_copy(
                                xnTh[:, dch * S + c * 128: dch * S + c * 128 + 128],
                                pst[:, sl * 512:sl * 512 + 128]), [("pe", pm)])
                            dl = R.dve(lambda e, sl=sl, dch=dch, c=c: e.scalar_tensor_tensor(
                                xnTl[:, dch * S + c * 128: dch * S + c * 128 + 128],
                                xnTh[:, dch * S + c * 128: dch * S + c * 128 + 128], -1.0,
                                pst[:, sl * 512:sl * 512 + 128], OP.mult, OP.add))
                            slot_ev[sl] = dl
                        used[i] = dl
            R.barrier()
            with ExitStack() as c2:
                sb2 = lambda n, sh, dt=F32: c2.enter_context(nc.sbuf_tensor(n, sh, dt))
                wc = [sb2(f"prw{i}", [128, nD * 128]) for i in range(2)]
                wch = [sb2(f"prh{i}", [128, nD * 128], BF16) for i in range(2)]
                wcl = [sb2(f"prl{i}", [128, nD * 128], BF16) for i in range(2)]
                ev = [sb2(f"prev{i}", [128, 512]) for i in range(2)]
                with nc.psum_tensor("prps", [128, 1024], F32) as psp:
                    evst = [None, None]
                    wc_used = [None, None]
                    wsplit = [None, None]
                    for W, out_d, width in ((wq, qs_d, DQ), (wk, ks_d, DQ), (wv, vs_d, D)):
                        for oc in range(width // 128):
                            iw = oc % 2
                            lw = None
                            for dc in range(nD):
                                lw = R.load(wc[iw][:, dc * 128:(dc + 1) * 128],
                                            W[dc * 128:(dc + 1) * 128, oc * 128:(oc + 1) * 128],
                                            [("dve", wsplit[iw])] if dc == 0 and wsplit[iw] is not None else [],
                                            ch=f"prw{iw}")
                            wt = [lw]
                            if wc_used[iw] is not None:
                                wt.append(("pe", wc_used[iw]))
                            R.dve(lambda e, iw=iw: e.tensor_copy(wch[iw][:, :], wc[iw][:, :]), wt)
                            wsplit[iw] = R.dve(lambda e, iw=iw: e.scalar_tensor_tensor(
                                wcl[iw][:, :], wch[iw][:, :], -1.0, wc[iw][:, :], OP.mult, OP.add))
                            for tsl in range(nTs):
                                b = tsl % 2
                                lastmm = None
                                for wop, xop in ((wch, xnTh), (wch, xnTl), (wcl, xnTh)):
                                    for dc in range(nD):
                                        wt = []
                                        if lastmm is None:
                                            wt = [("dve", wsplit[iw])]
                                            if evst[b] is not None:
                                                wt.append(("dve", evst[b][0]))
                                        first = lastmm is None
                                        lastmm = R.pe(lambda e, iw=iw, dc=dc, tsl=tsl, b=b,
                                                      wop=wop, xop=xop, first=first: e.matmul(
                                            psp[:, b * 512:b * 512 + 512],
                                            wop[iw][:, dc * 128:(dc + 1) * 128],
                                            xop[:, dc * S + tsl * 512: dc * S + (tsl + 1) * 512],
                                            start=first,
                                            stop=(wop is wcl and dc == nD - 1)), wt)
                                if tsl == nTs - 1:
                                    wc_used[iw] = lastmm
                                wt = [("pe", lastmm)]
                                if evst[b] is not None:
                                    wt.append(evst[b][1])
                                dl = R.dve(lambda e, b=b: e.tensor_copy(ev[b][:, :],
                                                                        psp[:, b * 512:b * 512 + 512]), wt)
                                st = R.store(out_d[oc * 128:(oc + 1) * 128, tsl * 512:(tsl + 1) * 512],
                                             ev[b][:, :], [("dve", dl)], ch=f"prev{b}")
                                evst[b] = (dl, st)
            R.barrier()
          if True:
            acc = cph.enter_context(nc.sbuf_tensor("at_acc", [128, S], F32))
            with ExitStack() as c2:
                sb2 = lambda n, sh, dt=F32: c2.enter_context(nc.sbuf_tensor(n, sh, dt))
                q1 = sb2("hq1", [128, S]); q2 = sb2("hq2", [128, S])
                k1 = sb2("hk1", [128, S]); k2 = sb2("hk2", [128, S])
                vv = sb2("hv", [128, S]); vT = sb2("hvT", [128, S])
                e1 = sb2("he1", [128, S]); e2 = sb2("he2", [128, S])
                wbp = [sb2(f"hw{i}", [128, S]) for i in range(2)]
                wT = sb2("hwT", [128, nC * 512])
                sqb = sb2("hsq", [128, 512])
                ob = [sb2(f"hob{i}", [128, 512]) for i in range(2)]
                obn = [sb2(f"hon{i}", [128, 512]) for i in range(2)]
                rr = sb2("hrr", [128, 8])
                PA = c2.enter_context(nc.psum_tensor("hpa", [128, 2048], F32))
                PB = c2.enter_context(nc.psum_tensor("hpb", [128, 2048], F32))
                PAv = PA.rearrange("p (c x) -> p c x", x=512)
                PBv = PB.rearrange("p (c x) -> p c x", x=512)
                wTv = wT.rearrange("p (c x) -> p c x", x=512)
                R.dve(lambda e: e.memset(acc[:, :], 0.0))
                obn_st = [None, None]
                ob_use = [None, None]
                pend = None
                a_e1_prev = None
                a_e2_prev = None
                o_ev = None
                wtr_ev = None
                wtr_pe = None
                omm_pe = {}
                head_gate = None

                def emit_wtr_and_omm():
                    nonlocal pend, wtr_ev, wtr_pe, o_ev
                    if pend is None:
                        return
                    h, qt, qg, dwc, wbuf = (pend[k] for k in ("h", "qt", "qg", "dw", "wbuf"))
                    for c4 in range(0, nC, 4):
                        for k in range(4):
                            c = c4 + k
                            wt = []
                            if c == 0:
                                wt = [("dve", dwc), ("act", a_e2_prev)]
                                gk = (h, qt - qg - 4) if qg == 0 else None
                                prevg = ((qt // 4) - 1)
                                if qg == 0 and (h, prevg) in omm_pe:
                                    pass
                            if k == 0 and wtr_ev is not None:
                                wt.append(wtr_ev)
                            pm = R.pe(lambda e, c=c, wbuf=wbuf: e.transpose(
                                PBv[:, c % 4, 0:128], wbuf[:, c * 128:(c + 1) * 128], idf[:, :]), wt)
                        wtr_pe = pm
                        wtr_ev = ("dve", R.dve(lambda e, c4=c4, qg=qg: e.tensor_copy(
                            wTv[:, c4:c4 + 4, qg * 128:qg * 128 + 128],
                            PBv[:, 0:4, 0:128]), [("pe", pm)]))
                    if qg == 3:
                        g = qt // 4
                        q0 = qt - 3
                        lastmm = None
                        for c in range(nC):
                            wt = [wtr_ev, ("act", a_e1_prev)] if c == 0 else []
                            if c == 0 and o_ev is not None:
                                wt.append(("dve", o_ev))
                            lastmm = R.pe(lambda e, c=c: e.matmul(
                                PA[:, 0:512], vT[:, c * 128:(c + 1) * 128],
                                wT[:, c * 512:(c + 1) * 512],
                                start=(c == 0), stop=(c == nC - 1)), wt)
                        omm_pe[(h, g)] = lastmm
                        bq = g % 2
                        wt = [("pe", lastmm)]
                        if ob_use[bq] is not None:
                            wt.append(("act", ob_use[bq]))
                        o_ev = R.dve(lambda e, bq=bq: e.tensor_copy(ob[bq][:, :], PA[:, 0:512]), wt)
                        R.dve(lambda e, bq=bq: e.tensor_tensor(sqb[:, :], ob[bq][:, :],
                                                               ob[bq][:, :], OP.mult))
                        R.dve(lambda e, q0=q0: e.tensor_tensor(acc[:, q0 * 128:q0 * 128 + 512],
                                                               acc[:, q0 * 128:q0 * 128 + 512],
                                                               sqb[:, :], OP.add))
                        wt = [("dve", o_ev)]
                        if obn_st[bq] is not None:
                            wt.append(obn_st[bq])
                        aon = R.act(lambda e, bq=bq, h=h: e.activation(
                            obn[bq][:, :], ob[bq][:, :], AF.Copy,
                            scale=nsf_cols[:, h:h + 1]), wt)
                        ob_use[bq] = aon
                        obn_st[bq] = R.store(oh_d[h * 128:(h + 1) * 128, q0 * 128:q0 * 128 + 512],
                                             obn[bq][:, :], [("act", aon)], ch=f"hon{bq}")
                    pend = None

                dw_prev = None
                for h in range(n_heads):
                    emit_wtr_and_omm()
                    wh = [("dve", head_gate)] if head_gate is not None else []
                    lq1 = R.load(q1[:, :], qs_d[h * 256:h * 256 + 128, :], wh, ch="hq1")
                    lq2 = R.load(q2[:, :], qs_d[h * 256 + 128:h * 256 + 256, :], wh, ch="hq2")
                    lk1 = R.load(k1[:, :], ks_d[h * 256:h * 256 + 128, :], wh, ch="hk1")
                    lk2 = R.load(k2[:, :], ks_d[h * 256 + 128:h * 256 + 256, :], wh, ch="hk2")
                    lv = R.load(vv[:, :], vs_d[h * 128:(h + 1) * 128, :], wh, ch="hv")
                    dvt = None
                    for c4 in range(0, nC, 4):
                        for k in range(4):
                            c = c4 + k
                            wt = []
                            if c == 0:
                                wt = [lv]
                                if wtr_ev is not None:
                                    wt.append(wtr_ev)
                                if a_e2_prev is not None:
                                    wt.append(("act", a_e2_prev))
                            pm = R.pe(lambda e, c=c: e.transpose(
                                PBv[:, c % 4, 0:128], vv[:, c * 128:(c + 1) * 128], idf[:, :]),
                                wt + ([("dve", dvt)] if k == 0 and dvt is not None else []))
                        dvt = R.dve(lambda e, c4=c4: e.tensor_copy(
                            vT[:, c4 * 128:(c4 + 4) * 128], PBv[:, 0:4, 0:128]), [("pe", pm)])
                    wtr_ev = ("dve", dvt)
                    for qt in range(nC):
                        qg = qt % 4
                        lastm1 = None
                        for tsl in range(nTs):
                            wt = []
                            if tsl == 0:
                                wt = [lq1, lk1]
                                if a_e1_prev is not None:
                                    wt.append(("act", a_e1_prev))
                                if o_ev is not None:
                                    wt.append(("dve", o_ev))
                            lastm1 = R.pe(lambda e, qt=qt, tsl=tsl: e.matmul(
                                PA[:, tsl * 512:(tsl + 1) * 512], q1[:, qt * 128:(qt + 1) * 128],
                                k1[:, tsl * 512:(tsl + 1) * 512], start=True, stop=True), wt)
                        lastm2 = None
                        for tsl in range(nTs):
                            wt = []
                            if tsl == 0:
                                wt = [lq2, lk2]
                                if wtr_ev is not None:
                                    wt.append(wtr_ev)
                                if a_e2_prev is not None:
                                    wt.append(("act", a_e2_prev))
                            lastm2 = R.pe(lambda e, qt=qt, tsl=tsl: e.matmul(
                                PB[:, tsl * 512:(tsl + 1) * 512], q2[:, qt * 128:(qt + 1) * 128],
                                k2[:, tsl * 512:(tsl + 1) * 512], start=True, stop=True), wt)
                        wt = [("pe", lastm1)]
                        if dw_prev is not None:
                            wt.append(("dve", dw_prev))
                        a_e1 = R.act(lambda e: e.activation(e1[:, :], PA[:, 0:S], AF.Exp, scale=SCALE32,
                                                            accum_out=rr[:, 0:1]), wt)
                        a_e1_prev = a_e1
                        a_e2 = R.act(lambda e: e.activation(e2[:, :], PB[:, 0:S], AF.Exp, scale=SCALE32,
                                                            accum_out=rr[:, 1:2]), [("pe", lastm2)])
                        a_e2_prev = a_e2
                        d = R.dve(lambda e: e.reciprocal(rr[:, 2:4], rr[:, 0:2]),
                                  [("act", a_e1), ("act", a_e2)])
                        d = R.dve(lambda e, h=h: e.tensor_scalar(rr[:, 3:4], rr[:, 3:4],
                                                                 lam_cols[:, h:h + 1], -1.0,
                                                                 OP.mult, OP.mult))
                        emit_wtr_and_omm()
                        ax1 = R.act(lambda e: e.activation(e1[:, :], e1[:, :], AF.Exp,
                                                           scale=rr[:, 2:3]), [("dve", d)])
                        ax2 = R.act(lambda e: e.activation(e2[:, :], e2[:, :], AF.Exp,
                                                           scale=rr[:, 3:4]))
                        wbuf = wbp[qt % 2]
                        wt = [("act", ax2)]
                        if wtr_pe is not None:
                            wt.append(("pe", wtr_pe))
                        dw = R.dve(lambda e, wbuf=wbuf: e.scalar_tensor_tensor(
                            wbuf[:, :], e1[:, :], 1.0, e2[:, :], OP.mult, OP.mult,
                            accum_out=rr[:, 4:5]), wt)
                        R.dve(lambda e: e.reciprocal(rr[:, 5:6], rr[:, 4:5]))
                        dw = R.dve(lambda e, wbuf=wbuf: e.tensor_scalar(wbuf[:, :], wbuf[:, :],
                                                                        rr[:, 5:6], None, OP.mult))
                        dw_prev = dw
                        pend = dict(h=h, qt=qt, qg=qg, dw=dw, wbuf=wbuf)
                    head_gate = R.cnt["dve"]
                emit_wtr_and_omm()
                head_gate = R.cnt["dve"]
            R.barrier()
            with ExitStack() as c2:
                with nc.psum_tensor("rsps", [128, 2048], F32) as pst:
                    dl = None
                    for c in range(nC):
                        wt = [("dve", dl)] if dl is not None and c >= 4 else []
                        pm = R.pe(lambda e, c=c: e.transpose(pst[:, (c % 4) * 512:(c % 4) * 512 + 128],
                                                             acc[:, c * 128:(c + 1) * 128], idf[:, :]), wt)
                        R.dve(lambda e, c=c: e.reduce_sum(tiny[:, 0:1],
                                                          pst[:, (c % 4) * 512:(c % 4) * 512 + 128],
                                                          axis=AX), [("pe", pm)])
                        aR = R.act(lambda e, c=c: e.activation(tiny2[:, 0:1], tiny[:, 0:1],
                                                               AF.Sqrt, bias=1e-6, scale=1.0 / D),
                                   [("dve", R.cnt["dve"])])
                        dl = R.dve(lambda e, c=c: e.reciprocal(rstd_cols[:, c:c + 1], tiny2[:, 0:1]),
                                   [("act", aR)])
            R.barrier()
            with ExitStack() as c2:
                sb2 = lambda n, sh, dt=F32: c2.enter_context(nc.sbuf_tensor(n, sh, dt))
                wos = c2.enter_context(nc.sbuf_tensor("bwos", [128, nD * D], F32))
                Ob = c2.enter_context(nc.sbuf_tensor("bOb", [128, nD * 512], F32))
                prj = [sb2(f"bpr{i}", [128, 512]) for i in range(2)]
                xtt = [sb2(f"bx{i}", [128, 512]) for i in range(2)]
                ott = [sb2(f"bt{i}", [128, 512]) for i in range(2)]
                PP = c2.enter_context(nc.psum_tensor("bpp", [128, 1024], F32))
                PT = c2.enter_context(nc.psum_tensor("bpt", [128, 1024], F32))
                lwo = None
                for fc in range(nD):
                    lwo = R.load(wos[:, fc * D:(fc + 1) * D], wo[fc * 128:(fc + 1) * 128, :], ch="bwo")
                ott_st = [None] * 4
                xtt_use = [None] * 4
                prj_use = [None, None]
                pt_use = [None, None]
                ob_ld_gate = None
                for qg in range(nTs):
                    lO = None
                    for fc in range(nD):
                        lO = R.load(Ob[:, fc * 512:(fc + 1) * 512],
                                    oh_d[fc * 128:(fc + 1) * 128, qg * 512:(qg + 1) * 512],
                                    [("pe", ob_ld_gate)] if ob_ld_gate is not None else [], ch="bob")
                    for d2c in range(nD):
                        b = d2c % 2
                        lastmm = None
                        for fc in range(nD):
                            wt = []
                            if fc == 0:
                                wt = [lO, lwo]
                                if prj_use[b] is not None:
                                    wt.append(("dve", prj_use[b]))
                            lastmm = R.pe(lambda e, fc=fc, d2c=d2c, b=b: e.matmul(
                                PP[:, b * 512:b * 512 + 512],
                                wos[:, fc * D + d2c * 128: fc * D + d2c * 128 + 128],
                                Ob[:, fc * 512:(fc + 1) * 512],
                                start=(fc == 0), stop=(fc == nD - 1)), wt)
                        if d2c == nD - 1:
                            ob_ld_gate = lastmm
                        dpr = R.dve(lambda e, b=b: e.tensor_copy(prj[b][:, :],
                                                                 PP[:, b * 512:b * 512 + 512]),
                                    [("pe", lastmm)])
                        prj_use[b] = dpr
                        tok0 = qg * 512
                        cx = R.load(xtt[b][:, :],
                                    src[tok0:tok0 + 512, d2c * 128:(d2c + 1) * 128]
                                    .rearrange("(a b) c -> b a c", b=128),
                                    [("dve", xtt_use[b])] if xtt_use[b] is not None else [],
                                    ch=f"bx{b}")
                        dub = None
                        for t4 in range(4):
                            c_tok = (tok0 + t4 * 128) // 128
                            sl = t4 % 2
                            pm = R.pe(lambda e, b=b, t4=t4, sl=sl: e.transpose(
                                PT[:, sl * 512:sl * 512 + 128],
                                prj[b][:, t4 * 128:(t4 + 1) * 128], idf[:, :]),
                                [("dve", dpr)] + ([("dve", pt_use[sl])] if pt_use[sl] is not None else []))
                            wt2 = [("pe", pm)]
                            if t4 == 0:
                                wt2.append(cx)
                                if ott_st[b] is not None:
                                    wt2.append(ott_st[b])
                            d1 = R.dve(lambda e, b=b, t4=t4, c_tok=c_tok, sl=sl: e.tensor_scalar(
                                ott[b][:, t4 * 128:(t4 + 1) * 128], PT[:, sl * 512:sl * 512 + 128],
                                rstd_cols[:, c_tok:c_tok + 1], None, OP.mult), wt2)
                            pt_use[sl] = d1
                            dub = R.dve(lambda e, b=b, t4=t4: e.tensor_tensor(
                                ott[b][:, t4 * 128:(t4 + 1) * 128],
                                ott[b][:, t4 * 128:(t4 + 1) * 128],
                                xtt[b][:, t4 * 128:(t4 + 1) * 128], OP.add))
                        xtt_use[b] = dub
                        ott_st[b] = R.store(dst[tok0:tok0 + 512, d2c * 128:(d2c + 1) * 128]
                                            .rearrange("(a b) c -> b a c", b=128),
                                            ott[b][:, :], [("dve", dub)], ch=f"bt{b}")
        R.barrier()

    ffn_phase(x_in, x1_d, True)
    attn_phase(x1_d, x2_d)
    ffn_phase(x2_d, y_out, False)
    R.op("t", lambda e: e.nop(),
         [(n, R.cnt[n]) for n in list(R.cnt) if n.startswith("st_") or n.startswith("ld_")])

    for n in list(R.cnt):
        sem[n] = ctx.enter_context(nc.semaphore("sem_" + n))
    blk = ctx.enter_context(nc.Block())
    last_wait = {}

    def replay(name):
        def go(e):
            for eng, fn, waits, inc, inc_n in R.steps:
                if eng != name:
                    continue
                for s, v in waits:
                    key = (name, s)
                    if last_wait.get(key, -1) < v:
                        e.wait_ge(sem[s], v)
                        last_wait[key] = v
                instr = fn(e)
                if inc is not None:
                    instr.then_inc(sem[inc], inc_n)
        return go

    blk.sync(replay("q"))
    blk.tensor(replay("t"))
    blk.vector(replay("v"))
    blk.scalar(replay("s"))
    blk.gpsimd(replay("g"))
    ctx.close()
    return nc


def kernel(L0_attn_x, L0_attn_Wq, L0_attn_Wk, L0_attn_Wv, L0_attn_Wo,
           L0_attn_norm_scale, L0_attn_lam, L0_ffn_w1, L0_ffn_w2):
    x = np.ascontiguousarray(np.asarray(L0_attn_x, dtype=np.float32))
    B = x.shape[0]
    nc = build_nc()
    common = {
        "w1": np.ascontiguousarray(L0_ffn_w1, dtype=np.float32),
        "w2": np.ascontiguousarray(L0_ffn_w2, dtype=np.float32),
        "wq": np.ascontiguousarray(L0_attn_Wq, dtype=np.float32),
        "wk": np.ascontiguousarray(L0_attn_Wk, dtype=np.float32),
        "wv": np.ascontiguousarray(L0_attn_Wv, dtype=np.float32),
        "wo": np.ascontiguousarray(L0_attn_Wo, dtype=np.float32),
        "ns": np.ascontiguousarray(L0_attn_norm_scale, dtype=np.float32),
        "lam": np.ascontiguousarray(L0_attn_lam, dtype=np.float32),
    }
    in_maps = [dict(common, x_in=np.ascontiguousarray(x[b])) for b in range(B)]
    res = run_bass_kernel_spmd(nc, in_maps, core_ids=list(range(B)))
    return np.stack([np.asarray(r["y_out"]) for r in res.results], axis=0)
